# version 02 (final)
# baseline (speedup 1.0000x reference)
import contextlib

import numpy as np
import ml_dtypes

import concourse.bass as bass
import concourse.mybir as mybir
from concourse.bass_utils import run_bass_kernel_spmd

F32 = mybir.dt.float32
BF16 = mybir.dt.bfloat16
AF = mybir.ActivationFunctionType
ALU = mybir.AluOpType

ENGINES = ("sync", "tensor", "vector", "scalar", "gpsimd")

D = 1024
DI = 2048
NST = 16
HALO = 3
TOK = 2048
T = HALO + TOK
CHUNKS = [(0, 3), (3, 512), (515, 512), (1027, 512), (1539, 512)]
NDT = DI // 128
NMT = D // 128
HEADS = 16
HD = 64
FM = 256
FF = 4096
FSL = 1024
CLAMP = 2.56e-4
GROUPS = [[0, 1, 2, 3], [4, 5, 6, 7]]

DEBUG_STAGE = None
SERIALIZE = False
SERIAL_ENGINES = ()
SERIAL_TAGS = ()


class Arena:
    def __init__(self, n, name):
        self.n = n; self.name = name
        self.segs = {}; self.free_list = [(0, n)]; self.hw = 0

    def alloc(self, name, length):
        assert name not in self.segs, name
        for i, (off, ln) in enumerate(self.free_list):
            if ln >= length:
                self.segs[name] = (off, length)
                if ln == length:
                    self.free_list.pop(i)
                else:
                    self.free_list[i] = (off + length, ln - length)
                self.hw = max(self.hw, off + length)
                return off
        raise MemoryError(f"{self.name}: no room for {name} ({length}); "
                          f"free={self.free_list} segs={list(self.segs)}")

    def free(self, name):
        off, ln = self.segs.pop(name)
        self.free_list.append((off, ln)); self.free_list.sort()
        out = []
        for s in self.free_list:
            if out and out[-1][0] + out[-1][1] == s[0]:
                out[-1] = (out[-1][0], out[-1][1] + s[1])
            else:
                out.append((s[0], s[1]))
        self.free_list = out


class PB:
    def __init__(self):
        self.q = {e: [] for e in ENGINES}
        self.cnt = {}
        self.lane = {}

    def sem(self, name):
        self.cnt[name] = 0

    def val(self, name):
        return self.cnt[name]

    def op(self, eng, make, waits=(), inc=None, by=1, tag=""):
        if (SERIALIZE or eng in SERIAL_ENGINES or
                any(tag.startswith(t) for t in SERIAL_TAGS)):
            waits = list(waits) + [(s, v) for s, v in self.cnt.items()
                                   if v > 0]
        w = tuple((s, v) for s, v in waits if v > 0)
        iv = None
        if inc is not None:
            self.cnt[inc] += by
            iv = (inc, by, self.cnt[inc])
        self.q[eng].append((w, iv, make, tag))
        return iv[2] if iv else None

    def dma(self, eng, out_ap, in_ap, waits=(), inc=None, tag="", slow=False):
        if inc is None:
            q = "dS" if eng == "sync" else "dG"
            i = self.lane.get(q, 0)
            self.lane[q] = (i + 1) % 8
            inc = f"{q}{i}"
        if slow:
            def mk(h):
                return h.dma_start(out=out_ap, in_=in_ap,
                                   allow_slow_non_contiguous=True)
        else:
            def mk(h):
                return h.dma_start(out=out_ap, in_=in_ap)
        v = self.op(eng, mk, waits=waits, inc=inc, by=16, tag=tag)
        return (inc, v)

    def dfence(self, q):
        return [(f"{q}{i}", self.cnt[f"{q}{i}"]) for i in range(8)
                if self.cnt[f"{q}{i}"] > 0]

    def check(self):
        pos = {e: 0 for e in ENGINES}
        cur = {s: 0 for s in self.cnt}
        total = sum(len(v) for v in self.q.values())
        done = 0
        progress = True
        while progress:
            progress = False
            for e in ENGINES:
                while pos[e] < len(self.q[e]):
                    w, iv, _, tag = self.q[e][pos[e]]
                    if any(cur[s] < v for s, v in w):
                        break
                    if iv:
                        cur[iv[0]] += iv[1]
                    pos[e] += 1; done += 1; progress = True
        if done != total:
            lines = []
            for e in ENGINES:
                if pos[e] < len(self.q[e]):
                    w, iv, _, tag = self.q[e][pos[e]]
                    blocked = [(s, v, cur[s]) for s, v in w if cur[s] < v]
                    lines.append(f"  {e}[{pos[e]}/{len(self.q[e])}] tag={tag}"
                                 f" blocked on {blocked}")
            raise RuntimeError("schedule deadlock:\n" + "\n".join(lines))

    def run(self, blk, sems):
        def body_for(eng_name):
            queue = self.q[eng_name]

            def body(h):
                last = {}
                for w, iv, make, tag in queue:
                    for s, v in w:
                        if last.get(s, -1) < v:
                            h.wait_ge(sems[s], v)
                            last[s] = v
                    inst = make(h)
                    if iv is not None:
                        inst.then_inc(sems[iv[0]], iv[1])
            return body
        blk.sync(body_for("sync"))
        blk.tensor(body_for("tensor"))
        blk.vector(body_for("vector"))
        blk.scalar(body_for("scalar"))
        blk.gpsimd(body_for("gpsimd"))


class Buf:

    def __init__(self):
        self.wv = {}
        self.rv = {}

    def w_waits(self):
        return list(self.rv.items()) + list(self.wv.items())

    def r_waits(self):
        return list(self.wv.items())

    def wrote(self, sem, val):
        self.rv = {}
        self.wv[sem] = max(self.wv.get(sem, 0), val)

    def read(self, sem, val):
        self.rv[sem] = max(self.rv.get(sem, 0), val)


class Psum:
    def __init__(self, ps):
        self.ps = ps
        self.b = [Buf() for _ in range(8)]

    def tile(self, bank, ncols=512, off=0, p0=0, p1=128):
        assert off + ncols <= 512, (off, ncols)
        return self.ps[p0:p1, bank * 512 + off:bank * 512 + off + ncols]

    def wait_w(self, bank):
        return self.b[bank].w_waits()

    def wait_r(self, bank):
        return self.b[bank].r_waits()

    def wrote(self, bank, sem, val):
        self.b[bank].wrote(sem, val)

    def read(self, bank, sem, val):
        self.b[bank].read(sem, val)


def build(debug_stage=None):
    nc = bass.Bass("TRN2", target_bir_lowering=False)
    stack = contextlib.ExitStack()

    def din(name, shape, dt=F32):
        return nc.dram_tensor(name, list(shape), dt, kind="ExternalInput").ap()

    def dint(name, shape, dt=F32, **kw):
        return nc.dram_tensor(name, list(shape), dt, kind="Internal", **kw).ap()

    x0T = din("x0T", (D, T))
    Wq = din("Wq", (D, D)); Wk = din("Wk", (D, D))
    Wv = din("Wv", (D, D)); Wo = din("Wo", (D, D))
    bq = din("bq", (D,)); bk = din("bk", (D,))
    bv = din("bv", (D,)); bo = din("bo", (D,))
    RMi = din("RM", (HD, FM))
    inwT = din("inwT", (D, 2 * DI))
    convw = din("convw", (DI, 4)); convb = din("convb", (DI,))
    xprojT = din("xprojT", (DI, 96))
    dtwT = din("dtwT", (HD, DI)); dtb = din("dtb", (DI,))
    Alog = din("Alog", (DI, NST)); Dvec = din("Dvec", (DI,))
    outwT = din("outwT", (DI, D))
    gw1T = din("gw1T", (D, D)); gw2T = din("gw2T", (D, 8))
    gb1 = din("gb1", (D,)); gb2i = din("gb2", (8,))
    W1Ts = din("W1Ts", (D, FSL)); W2Ts = din("W2Ts", (FSL, D))
    TW1Ts = din("TW1Ts", (8, D, FSL)); TW2Ts = din("TW2Ts", (8, FSL, D))
    b1i = din("b1", (FF,)); b2i = din("b2", (D,))
    Tb1i = din("Tb1", (8, FF)); Tb2i = din("Tb2", (8, D))
    halom = din("halom", (128, 4))
    seli = din("sel", (128, 4))
    I128 = din("I128", (128, 128))

    x5T = nc.dram_tensor("x5T", [D, TOK], F32, kind="ExternalOutput").ap()

    winw = dint("w_inw", (D, 2 * DI), BF16)
    woutw = dint("w_outw", (DI, D), BF16)
    wg1 = dint("w_g1", (D, D), BF16)
    wqkvo = dint("w_qkvo", (4, D, D), BF16)
    wtw1 = dint("w_tw1", (8, D, FSL), BF16)
    wtw2 = dint("w_tw2", (8, FSL, D), BF16)
    ww1 = dint("w_w1", (D, FSL), BF16)
    ww2 = dint("w_w2", (FSL, D), BF16)
    dsp = dint("dsp", (DI, T))
    xcsp = dint("xcsp", (DI, T))
    sresp = dint("sresp", (DI, T))
    qpsp = dint("qpsp", (HEADS * FM, T), BF16)
    x3sp = dint("x3sp", (D, T))
    x4sp = dint("x4sp", (D, T))
    gysp = dint("gysp", (DI, T), BF16)
    bcsp = dint("bcsp", (2, NST, T))
    rcpsp = dint("rcpsp", (HEADS, 512))
    mbsp = dint("mbsp", (2, FF + D))
    kv_i = dint("kv_i", (128, 32 * 65))
    kv_o = dint("kv_o", (128, 32 * 65))
    g_i = [dint(f"g_i{i}", (16,)) for i in range(2)]
    g_o = [dint(f"g_o{i}", (16,)) for i in range(2)]
    MWPAY = D * FSL * 2 + FSL
    mw_i = [dint(f"mw_i{i}", (MWPAY,), BF16) for i in range(2)]
    mw_o = [dint(f"mw_o{i}", (4, MWPAY), BF16) for i in range(2)]
    cv_i = dint("cv_i", (DI * NST * 2,))
    cv_o = dint("cv_o", (4, DI * NST * 2))

    ARN = 43200
    ar_t = stack.enter_context(nc.sbuf_tensor("arena", [128, ARN], F32))
    ar = ar_t[:, :]
    A = Arena(ARN, "arena")

    def af(name, n):
        o = A.alloc(name, n)
        return ar[:, o:o + n]

    def abf(name, nbf):
        n = (nbf + 1) // 2
        o = A.alloc(name, n)
        return ar[:, o:o + n].bitcast(BF16)

    sm_t = stack.enter_context(nc.sbuf_tensor("smalls", [128, 2100], F32))
    sm = sm_t[:, :]
    SM = Arena(2100, "smalls")

    def asm(name, n):
        o = SM.alloc(name, n)
        return sm[:, o:o + n]

    ps_t = stack.enter_context(nc.psum_tensor("psum", [128, 4096], F32))
    ps = ps_t[:, :]
    P = Psum(ps)

    xdbl_t = stack.enter_context(nc.sbuf_tensor("xdbl", [96, T], F32))
    xdbl = xdbl_t[:, :]
    dtw_t = stack.enter_context(nc.sbuf_tensor("dtwsb", [64, DI], BF16))
    dtwsb = dtw_t[:, :]
    rm_t = stack.enter_context(nc.sbuf_tensor("rmsb", [128, FM], BF16))
    rmsb = rm_t[:, :]
    onesr_t = stack.enter_context(nc.sbuf_tensor("onesrow", [1, 130], F32))
    onesrow = onesr_t[:, :]
    bonesr_t = stack.enter_context(nc.sbuf_tensor("bonesrow", [1, 2080], BF16))
    bonesrow = bonesr_t[:, :]
    wstg_t = stack.enter_context(nc.sbuf_tensor("wstg", [128, 4096], BF16))
    wstg = wstg_t[:, :]

    pb = PB()
    for s in ("Tn", "Vc", "Ac", "Gp", "CC"):
        pb.sem(s)
    for q in ("dS", "dG"):
        for i in range(8):
            pb.sem(f"{q}{i}")

    def mm(out_ap, lh, rh, start, stop, waits=(), inc=None, tag=""):
        return pb.op(
            "tensor",
            lambda h: h.matmul(out_ap, lh, rh, start=start, stop=stop),
            waits=waits, inc=inc, tag=tag)

    onescol = asm("ones", 1)
    onescolb = asm("onesb", 1).bitcast(BF16)[:, 0:1]
    Acol = asm("Acol", NDT * NST)
    Acolv = Acol.rearrange("p (d n) -> p d n", d=NDT)
    cwcol = asm("cw", NDT * 4)
    cwv = cwcol.rearrange("p (d j) -> p d j", d=NDT)
    cbcol = asm("cb", NDT)
    Dcol = asm("Dv", NDT)
    dtbcol = asm("dtb", NDT)
    bqc = asm("bq", NMT); bkc = asm("bk", NMT)
    boc = asm("bo", NMT); gb1c = asm("gb1", NMT)
    halomc = asm("halom", 4)
    selc = asm("sel", 4)
    gcol = [asm(f"g{i}", 8) for i in range(2)]
    mb1c = asm("mb1", FF // 128)
    mb2c = asm("mb2", D // 128)
    hinc = asm("hin", NDT * NST)
    hinv = hinc.rearrange("p (d n) -> p d n", d=NDT)
    vagg = asm("vagg", NDT * NST)
    vaggv = vagg.rearrange("p (d n) -> p d n", d=NDT)
    ssum = asm("ssum", NDT)
    pcol = asm("pcore", NDT * NST)
    pcolv = pcol.rearrange("p (d n) -> p d n", d=NDT)
    carry3 = asm("carry3", NDT * 3)
    carry3v = carry3.rearrange("p (d j) -> p d j", d=NDT)
    I128b = asm("I128b", 64).bitcast(BF16)
    gb2rep = asm("gb2rep", 8)
    gw2sb = asm("gw2b", 32).bitcast(BF16).rearrange("p (k e) -> p k e", k=8)
    CONSTVALS = [1e-5, 1.0 / D, -0.5, -1.0, 0.5, 2.0, 1.0 / 8192.0]
    constc = asm("constc", len(CONSTVALS))
    for i, v in enumerate(CONSTVALS):
        nc.const_aps.register(F32, v, constc[:, i:i + 1])

    xres = af("xres", NMT * T)
    xresv = xres.rearrange("p (d t) -> p d t", d=NMT)
    gisb = asm("gisb", 576).bitcast(BF16)
    B_gi = Buf()
    gtmp8 = asm("gtmp8", 1)
    grepc = asm("grepc", 8)
    gb2col8 = asm("gb2c8", 1)
    gcol8 = [asm(f"gc8_{i}", 1) for i in range(2)]
    BROWN = 2051 + 1024 + 5120
    brow = abf("brow", BROWN)[0:1, :]
    BRA = Arena(BROWN, "brow")
    nmr_o = BRA.alloc("nmr", 2051)
    bv_o = BRA.alloc("bvrow", 1024)
    WSPOOL = BRA.alloc("wspool", 5120)

    def brs(off, n):
        return brow[0:1, off:off + n]

    bvrow = brs(bv_o, 1024)

    def nmr(c0, cl):
        return brs(nmr_o + c0, cl)

    xlnc = abf("xlnc", NMT * 512)
    xlncv = xlnc.rearrange("p (d t) -> p d t", d=NMT)
    rowsc_full = af("rowsc", 3 * 512)
    rowsc = rowsc_full[0:1, :]
    sqb = [af("sq0", 512), af("sq1", 512)]
    B_xln = Buf()
    B_rows = Buf()
    B_sq = [Buf(), Buf()]
    B_nmrb = Buf()

    def emit_ln_chunk(ci, c0, cl, deps, tag, src=None):
        srcv = xresv if src is None else src
        for d in range(NMT):
            sl = d % 2
            w = list(deps) if d == 0 else []
            av = pb.op("scalar",
                       lambda h, d=d, sl=sl: h.activation(
                           out=sqb[sl][:, 0:cl], in_=srcv[:, d, c0:c0 + cl],
                           func=AF.Square),
                       waits=w + B_sq[sl].w_waits(), inc="Ac",
                       tag=f"{tag}sq{d}")
            B_sq[sl].wrote("Ac", av)
            wS = (list(deps) + P.wait_w(0) + P.wait_w(1)) if d == 0 else []
            mm(P.tile(0, cl, p0=0, p1=1), onescol, srcv[:, d, c0:c0 + cl],
               start=(d == 0), stop=(d == NMT - 1), waits=wS,
               tag=f"{tag}sx{d}")
            tv = mm(P.tile(1, cl, p0=0, p1=1), onescol, sqb[sl][:, 0:cl],
                    start=(d == 0), stop=(d == NMT - 1),
                    waits=[("Ac", av)], inc="Tn", tag=f"{tag}sxx{d}")
            B_sq[sl].read("Tn", tv)
        P.wrote(0, "Tn", tv); P.wrote(1, "Tn", tv)
        av = pb.op("scalar",
                   lambda h: h.activation(
                       out=rowsc[0:1, 0:cl], in_=P.tile(0, cl, p0=0, p1=1),
                       func=AF.Identity, scale=1.0 / D),
                   waits=[("Tn", tv)] + B_rows.w_waits(), inc="Ac",
                   tag=f"{tag}m")
        av2 = pb.op("scalar",
                    lambda h: h.activation(
                        out=rowsc[0:1, 512:512 + cl],
                        in_=P.tile(1, cl, p0=0, p1=1),
                        func=AF.Identity, scale=1.0 / D),
                    waits=[("Tn", tv)], inc="Ac", tag=f"{tag}ms")
        P.read(0, "Ac", av2); P.read(1, "Ac", av2)
        vv = pb.op("vector",
                   lambda h: h.scalar_tensor_tensor(
                       out=rowsc[0:1, 512:512 + cl],
                       in0=rowsc[0:1, 0:cl], scalar=-1.0,
                       in1=rowsc[0:1, 0:cl], op0=ALU.mult, op1=ALU.mult)
                   if False else h.tensor_mul(
                       rowsc[0:1, 1024:1024 + cl],
                       rowsc[0:1, 0:cl], rowsc[0:1, 0:cl]),
                   waits=[("Ac", av2)], inc="Vc", tag=f"{tag}mm")
        vv = pb.op("vector",
                   lambda h: h.tensor_sub(
                       rowsc[0:1, 512:512 + cl], rowsc[0:1, 512:512 + cl],
                       rowsc[0:1, 1024:1024 + cl]),
                   waits=[("Vc", vv)], inc="Vc", tag=f"{tag}var")
        av = pb.op("scalar",
                   lambda h: h.activation(
                       out=rowsc[0:1, 1024:1024 + cl],
                       in_=rowsc[0:1, 512:512 + cl], func=AF.Ln, bias=1e-5),
                   waits=[("Vc", vv)], inc="Ac", tag=f"{tag}lg")
        av = pb.op("scalar",
                   lambda h: h.activation(
                       out=rowsc[0:1, 1024:1024 + cl],
                       in_=rowsc[0:1, 1024:1024 + cl], func=AF.Exp,
                       scale=-0.5),
                   waits=[("Ac", av)], inc="Ac", tag=f"{tag}ex")
        B_rows.wrote("Ac", av)
        nv = pb.op("vector",
                   lambda h: h.scalar_tensor_tensor(
                       out=nmr(c0, cl), in0=rowsc[0:1, 0:cl], scalar=-1.0,
                       in1=rowsc[0:1, 1024:1024 + cl],
                       op0=ALU.mult, op1=ALU.mult),
                   waits=[("Ac", av)] + B_nmrb.w_waits(), inc="Vc",
                   tag=f"{tag}nm")
        B_nmrb.wrote("Vc", nv)
        B_rows.read("Vc", nv)
        tv2 = mm(P.tile(2, cl), onesrow[0:1, 0:128],
                 rowsc[0:1, 1024:1024 + cl], True, True,
                 waits=[("Ac", av)] + P.wait_w(2), inc="Tn", tag=f"{tag}rr")
        P.wrote(2, "Tn", tv2)
        B_rows.read("Tn", tv2)
        for d in range(NMT):
            vv = pb.op("vector",
                       lambda h, d=d: h.tensor_mul(
                           xlncv[:, d, 0:cl], srcv[:, d, c0:c0 + cl],
                           P.tile(2, cl)),
                       waits=[("Tn", tv2)] + (B_xln.w_waits() if d == 0
                                              else []),
                       inc="Vc", tag=f"{tag}xl{d}")
        P.read(2, "Vc", vv)
        B_xln.wrote("Vc", vv)
        return vv, nv

    def emit_setup():
        for d in range(NMT):
            pb.dma("sync", xresv[:, d, :], x0T[d * 128:(d + 1) * 128, :],
                   tag=f"x0_{d}")
        pb.op("vector", lambda h: h.memset(onescol, 1.0), inc="Vc",
              tag="ones")
        for i, v in enumerate(CONSTVALS):
            pb.op("vector", lambda h, i=i, v=v: h.memset(constc[:, i:i + 1], v),
                  inc="Vc", tag=f"cst{i}")
        pb.op("vector", lambda h: h.memset(onescolb, 1.0), inc="Vc",
              tag="onesb")
        pb.op("vector", lambda h: h.memset(onesrow[:, :], 1.0), inc="Vc",
              tag="onesrow")
        pb.op("vector", lambda h: h.memset(bonesrow[:, :], 1.0), inc="Vc",
              tag="bonesrow")

        def colload(dst, src, ntile, tag):
            pb.dma("sync", dst[:, 0:ntile],
                   src.rearrange("(t p) -> p t", p=128), tag=tag, slow=True)
        colload(bqc, bq, NMT, "bq"); colload(bkc, bk, NMT, "bk")
        colload(boc, bo, NMT, "bo"); colload(gb1c, gb1, NMT, "gb1")
        colload(cbcol, convb, NDT, "cb"); colload(Dcol, Dvec, NDT, "Dv")
        colload(dtbcol, dtb, NDT, "dtbc")
        pb.dma("sync", halomc, halom, tag="halomld")
        pb.dma("sync", selc, seli, tag="selld")
        pb.dma("sync", gb2rep, gb2i.partition_broadcast(128), tag="gb2ld")
        pb.dma("sync", gb2col8[0:8, 0:1], gb2i.unsqueeze(1), tag="gb2cld")
        pb.dma("gpsimd", I128b, I128, tag="i128ld")
        pb.dma("gpsimd", gw2sb, gw2T.rearrange("(k p) e -> p k e", p=128),
               tag="gw2ld")
        pb.dma("sync", Acolv, Alog.rearrange("(d p) n -> p d n", p=128),
               tag="alogld")
        pb.dma("sync", cwv, convw.rearrange("(d p) j -> p d j", p=128),
               tag="convwld")
        pb.dma("gpsimd", rmsb[0:64, :], RMi, tag="rmld")
        pb.dma("gpsimd", rmsb[64:128, :], RMi, tag="rmld2")
        pb.dma("gpsimd", dtwsb, dtwT, tag="dtwld")
        pb.dma("gpsimd", bvrow, bv.unsqueeze(0), tag="bvrowld")
        av = pb.op("scalar",
                   lambda h: h.activation(out=Acol, in_=Acol, func=AF.Exp),
                   waits=pb.dfence("dS") + pb.dfence("dG"), inc="Ac", tag="aexp")
        pb.op("vector", lambda h: h.tensor_scalar_mul(Acol, Acol, -1.0),
              waits=[("Ac", av)], inc="Vc", tag="aneg")
        return pb.dfence("dS") + pb.dfence("dG"), pb.val("Vc")

    def emit_wcast(const_v, jobs):
        stgv = wstg.rearrange("p (b c) -> p b c", b=2)
        free_v = [None, None]
        mmread = [0, 0]
        for srcf, dstf, nk, width, ws_off, tag in jobs:
            nparts = (width + 511) // 512
            assert ws_off is None or nparts <= 8
            ws_tvs = {}
            npieces = (width + 2047) // 2048
            for k in range(nk):
                for pc in range(npieces):
                    co = pc * 2048
                    pwid = min(2048, width - co)
                    sl = (k * npieces + pc) % 2
                    w = [free_v[sl]] if free_v[sl] else []
                    if mmread[sl]:
                        w = w + [("Tn", mmread[sl])]
                    lv = pb.dma("gpsimd", stgv[:, sl, 0:pwid],
                                srcf(k)[:, co:co + pwid],
                                waits=w, tag=f"wc_{tag}i{k}_{pc}")
                    if ws_off is not None:
                        for lp in range((pwid + 511) // 512):
                            part = (co // 512) + lp
                            pw = min(512, pwid - lp * 512)
                            wk = [lv, ("Vc", const_v)]
                            if k == 0:
                                wk += P.wait_w(part)
                            tv = mm(P.tile(part, pw, p0=0, p1=1), onescolb,
                                    stgv[:, sl, lp * 512:lp * 512 + pw],
                                    start=(k == 0), stop=(k == nk - 1),
                                    waits=wk, inc="Tn",
                                    tag=f"ws_{tag}_{k}_{part}")
                            mmread[sl] = tv
                            if k == nk - 1:
                                ws_tvs[part] = (pw, tv)
                                P.wrote(part, "Tn", tv)
                    free_v[sl] = pb.dma("gpsimd", dstf(k)[:, co:co + pwid],
                                        stgv[:, sl, 0:pwid],
                                        waits=[lv], tag=f"wc_{tag}o{k}_{pc}")
            for part, (pw, tv) in ws_tvs.items():
                vv = pb.op("vector",
                           lambda h, ws_off=ws_off, part=part, pw=pw:
                           h.tensor_copy(
                               brs(ws_off + part * 512, pw),
                               P.tile(part, pw, p0=0, p1=1)),
                           waits=[("Tn", tv)], inc="Vc",
                           tag=f"wse_{tag}{part}")
                P.read(part, "Vc", vv)
        return pb.dfence("dG"), pb.val("Vc")

    def wjob(W, dst, nk, width, ws_off, tag):
        return (lambda k: W[k * 128:(k + 1) * 128, :],
                lambda k: dst[k * 128:(k + 1) * 128, :], nk, width, ws_off,
                tag)

    def emit_attn_a(wv, x0v, cstv, wsumv):
        wdb = abf("awdb", 2 * NMT * D)
        wdbv = wdb.rearrange("p (s k c) -> p s k c", s=2, k=NMT)
        B_wdb = [Buf(), Buf()]
        qsb = abf("qsb", NMT * 512)
        qsbv = qsb.rearrange("p (d t) -> p d t", d=NMT)
        fst = abf("fst", 4 * 512)
        fstv = fst.rearrange("p (j t) -> p j t", j=4)
        vsb = abf("vsb", 4 * 16 * 65)
        vsbv = vsb.rearrange("p (s h c) -> p s h c", s=4, h=16)
        ktm = abf("ktm", 4 * FM)
        ktmv = ktm.rearrange("p (s c) -> p s c", s=4)
        er = af("er", 4 * 512)
        kvacc = af("kvacc", 32 * 65)
        kvaccv = kvacc.rearrange("p (t c) -> p t c", t=32)
        B_qsb = [Buf() for _ in range(NMT)]
        B_fst = Buf()
        B_vsb = Buf()
        B_ktm = [Buf() for _ in range(4)]
        B_er = [Buf(), Buf()]

        wstep = [0]

        def load_w(widx, tag):
            sl = wstep[0] % 2
            wstep[0] += 1
            w = B_wdb[sl].w_waits() + wv
            lv = pb.dma("sync", wdbv[:, sl, :, :],
                        wqkvo[widx].rearrange("(k p) c -> p k c", p=128),
                        waits=w, tag=tag)
            B_wdb[sl].wrote(*lv)
            return sl, lv

        def proj_fm(ci, c0, cl, slot, wlv, wsrow_off, bias, lnv, tag):
            evs = []
            lastmm = None
            for m in range(NMT):
                bank = m
                w0 = P.wait_w(bank) + [wlv, ("Vc", lnv)]
                for k in range(NMT):
                    mm(P.tile(bank, cl),
                       wdbv[:, slot, k, m * 128:(m + 1) * 128],
                       xlncv[:, k, 0:cl], start=(k == 0), stop=False,
                       waits=w0 if k == 0 else [], tag=f"{tag}{ci}m{m}k{k}")
                tv = mm(P.tile(bank, cl),
                        brs(wsrow_off + m * 128, 128),
                        nmr(c0, cl), start=False, stop=True, inc="Tn",
                        waits=[("Vc", wsumv)] if m == 0 else [],
                        tag=f"{tag}{ci}m{m}r")
                P.wrote(bank, "Tn", tv)
                lastmm = tv
                av = pb.op("scalar",
                           lambda h, m=m, bank=bank, bias=bias:
                           h.activation(out=qsbv[:, m, 0:cl],
                                        in_=P.tile(bank, cl),
                                        func=AF.Identity,
                                        bias=bias[:, m:m + 1]),
                           waits=[("Tn", tv)] + B_qsb[m].w_waits(),
                           inc="Ac", tag=f"{tag}{ci}e{m}")
                P.read(bank, "Ac", av)
                B_qsb[m].wrote("Ac", av)
                evs.append(av)
            return evs, lastmm

        def feats(src_ap, dst_ap, bank, par, extraw, tag):
            clx = src_ap.shape[-1]
            a1 = pb.op("scalar",
                       lambda h: h.activation(
                           out=er[:, par * 1024:par * 1024 + clx],
                           in_=src_ap, func=AF.Exp),
                       waits=extraw + B_er[par].w_waits(), inc="Ac",
                       tag=f"{tag}e")
            a2 = pb.op("scalar",
                       lambda h: h.activation(
                           out=er[:, par * 1024 + 512:par * 1024 + 512 + clx],
                           in_=src_ap, func=AF.Relu),
                       waits=[], inc="Ac", tag=f"{tag}r")
            B_er[par].wrote("Ac", a2)
            P.read(bank, "Ac", a2)
            vv = pb.op("vector",
                       lambda h: h.scalar_tensor_tensor(
                           out=dst_ap,
                           in0=er[:, par * 1024:par * 1024 + clx],
                           scalar=1.0,
                           in1=er[:, par * 1024 + 512:par * 1024 + 512 + clx],
                           op0=ALU.min, op1=ALU.add),
                       waits=[("Ac", a2)], inc="Vc", tag=f"{tag}s")
            B_er[par].read("Vc", vv)
            return vv

        first_kv = True
        for ci, (c0, cl) in enumerate(CHUNKS):
            lnv, _ = emit_ln_chunk(ci, c0, cl,
                                   x0v + [("Vc", cstv)], f"A{ci}")
            sl, wlv = load_w(0, f"awq{ci}")
            evs, lastq = proj_fm(ci, c0, cl, sl, wlv, WSPOOL + 1024, bqc, lnv, "q")
            B_wdb[sl].read("Tn", lastq)
            if debug_stage == "qsb" and ci == 1:
                dbg = nc.dram_tensor("dbg", [D, 512], BF16,
                                     kind="ExternalOutput").ap()
                dsv = pb.dma("sync", dbg.rearrange("(d p) t -> p d t", p=128),
                             qsbv[:, :, 0:cl], waits=[("Ac", evs[-1])],
                             tag="dbgqsb")
                pb.op("gpsimd", lambda h: h.engine_nop(),
                      waits=[dsv], tag="qsbfinal")
            for hg in range(8):
                lastf = None
                for hh in range(2):
                    h_ = hg * 2 + hh
                    mt = h_ // 2
                    poff = (h_ % 2) * 64
                    for mh in range(2):
                        j = hh * 2 + mh
                        bank = 4 + j
                        tv = mm(P.tile(bank, cl),
                                rmsb[poff:poff + 64,
                                     mh * 128:(mh + 1) * 128],
                                qsbv[poff:poff + 64, mt, 0:cl],
                                True, True,
                                waits=P.wait_w(bank) + [("Ac", evs[mt])],
                                inc="Tn", tag=f"qf{ci}h{h_}m{mh}")
                        P.wrote(bank, "Tn", tv)
                        B_qsb[mt].read("Tn", tv)
                        lastf = feats(P.tile(bank, cl), fstv[:, j, 0:cl],
                                      bank, j % 2,
                                      [("Tn", tv)] +
                                      (B_fst.w_waits() if j == 0 else []),
                                      f"qx{ci}h{h_}m{mh}")
                B_fst.wrote("Vc", lastf)
                sv = pb.dma("sync",
                            qpsp.rearrange("(j p) t -> p j t", p=128)[
                                :, hg * 4:hg * 4 + 4, c0:c0 + cl],
                            fstv[:, :, 0:cl], waits=[("Vc", lastf)],
                            tag=f"qsp{ci}g{hg}")
                B_fst.read(*sv)
            if cl == 3:
                continue
            sl, wlv = load_w(1, f"awk{ci}")
            evs, lastk = proj_fm(ci, c0, cl, sl, wlv, WSPOOL + 2048, bkc,
                                 lnv, "k")
            B_wdb[sl].read("Tn", lastk)
            slv, wlv2 = load_w(2, f"awv{ci}")
            nsub = cl // 128
            lastvmm = None
            for sub in range(nsub):
                t0r = sub * 128
                for half in range(2):
                    bank = sub * 2 + half
                    w0 = P.wait_w(bank) + [wlv2, ("Vc", lnv)]
                    for k in range(NMT):
                        mm(P.tile(bank, 512),
                           xlncv[:, k, t0r:t0r + 128],
                           wdbv[:, slv, k, half * 512:half * 512 + 512],
                           start=(k == 0), stop=False,
                           waits=w0 if k == 0 else [],
                           tag=f"v{ci}s{sub}h{half}k{k}")
                    mm(P.tile(bank, 512), nmr(c0 + t0r, 128),
                       brs(WSPOOL + 3072 + half * 512, 512),
                       start=False, stop=False, tag=f"v{ci}s{sub}h{half}r")
                    tv = mm(P.tile(bank, 512), bonesrow[0:1, 0:128],
                            bvrow[0:1, half * 512:half * 512 + 512],
                            start=False, stop=True, inc="Tn",
                            tag=f"v{ci}s{sub}h{half}b")
                    P.wrote(bank, "Tn", tv)
                    lastvmm = tv
                    vv = pb.op("vector",
                               lambda h, sub=sub, half=half, bank=bank:
                               h.tensor_copy(
                                   vsbv[:, sub, half * 8:half * 8 + 8, 0:64],
                                   P.tile(bank, 512).rearrange(
                                       "p (hh c) -> p hh c", hh=8)),
                               waits=[("Tn", tv)] + B_vsb.w_waits(),
                               inc="Vc", tag=f"ve{ci}s{sub}h{half}")
                    P.read(bank, "Vc", vv)
            B_wdb[slv].read("Tn", lastvmm)
            vv1 = pb.op("vector",
                        lambda h: h.memset(vsbv[:, :, :, 64:65], 1.0),
                        waits=[("Vc", vv)], inc="Vc", tag=f"vone{ci}")
            B_vsb.wrote("Vc", vv1)
            for h_ in range(HEADS):
                mt = h_ // 2
                poff = (h_ % 2) * 64
                for sub in range(nsub):
                    bank = sub % 4
                    tv = mm(P.tile(bank, FM),
                            qsbv[poff:poff + 64, mt,
                                 sub * 128:sub * 128 + 128],
                            rmsb[poff:poff + 64, :], True, True,
                            waits=P.wait_w(bank) + [("Ac", evs[mt])],
                            inc="Tn", tag=f"kf{ci}h{h_}s{sub}")
                    P.wrote(bank, "Tn", tv)
                    B_qsb[mt].read("Tn", tv)
                    vv = feats(P.tile(bank, FM), ktmv[:, sub, :], bank,
                               sub % 2,
                               [("Tn", tv)] + B_ktm[sub].w_waits(),
                               f"kx{ci}h{h_}s{sub}")
                    B_ktm[sub].wrote("Vc", vv)
                for mh in range(2):
                    bank = 4 + mh
                    w0 = (P.wait_w(bank) + B_vsb.r_waits() +
                          [("Vc", pb.val("Vc"))])
                    tv = None
                    for sub in range(nsub):
                        tv = mm(P.tile(bank, 65),
                                ktmv[:, sub, mh * 128:mh * 128 + 128],
                                vsbv[:, sub, h_, :],
                                start=(sub == 0), stop=(sub == nsub - 1),
                                waits=w0 if sub == 0 else [],
                                inc="Tn" if sub == nsub - 1 else None,
                                tag=f"kv{ci}h{h_}m{mh}")
                    P.wrote(bank, "Tn", tv)
                    for sub in range(nsub):
                        B_ktm[sub].read("Tn", tv)
                    j = h_ * 2 + mh
                    if first_kv:
                        vv = pb.op("vector",
                                   lambda h, j=j, bank=bank: h.tensor_copy(
                                       kvaccv[:, j, :], P.tile(bank, 65)),
                                   waits=[("Tn", tv)], inc="Vc",
                                   tag=f"kva{ci}h{h_}m{mh}")
                    else:
                        vv = pb.op("vector",
                                   lambda h, j=j, bank=bank: h.tensor_add(
                                       kvaccv[:, j, :], kvaccv[:, j, :],
                                       P.tile(bank, 65)),
                                   waits=[("Tn", tv)], inc="Vc",
                                   tag=f"kva{ci}h{h_}m{mh}")
                    P.read(bank, "Vc", vv)
            first_kv = False
            B_vsb.read("Tn", pb.val("Tn"))
        lastv = pb.val("Vc")
        sv = pb.dma("sync", kv_i, kvacc, waits=[("Vc", lastv)], tag="kvout")
        cv = pb.op("gpsimd",
                   lambda h: h.collective_compute(
                       "AllReduce", ALU.add, ins=[kv_i[:, :]],
                       outs=[kv_o[:, :]], replica_groups=GROUPS),
                   waits=[sv], inc="CC", tag="kvAR")
        fence1 = ([("Tn", pb.val("Tn")), ("Vc", pb.val("Vc")),
                   ("Ac", pb.val("Ac"))] + pb.dfence("dS") +
                  pb.dfence("dG"))
        for n in ("awdb", "qsb", "fst", "vsb", "ktm", "er", "kvacc"):
            A.free(n)
        wos = abf("wos", NMT * D)
        wov = wos.rearrange("p (k c) -> p k c", k=NMT)
        lo = pb.dma("sync", wov, wqkvo[3].rearrange("(k p) c -> p k c", p=128),
                    waits=fence1 + wv, tag="ldwo")
        qpc = abf("qpc", 32 * 256)
        qpcv = qpc.rearrange("p (j t) -> p j t", j=32)
        PIECES = [(0, 3)] + [(3 + 256 * i, 256) for i in range(8)]
        kvb = abf("kvb", 32 * 128)
        kvbv = kvb.rearrange("p (j c) -> p j c", j=32)
        ksel = abf("ksel", 32 * 16)
        kselv = ksel.rearrange("p (j c) -> p j c", j=32)
        attnb = abf("attnb", NMT * 512)
        attnbv = attnb.rearrange("p (d t) -> p d t", d=NMT)
        rep = af("rep", 2 * 512)
        nrm = rowsc_full[:, 0:512]
        kvf = af("kvf", 32 * 65)
        kvfv = kvf.rearrange("p (j c) -> p j c", j=32)
        B_qpc = Buf(); B_attnb = Buf(); B_rep = [Buf(), Buf()]
        B_nrm = Buf(); B_rcp = Buf()

        lv2 = pb.dma("sync", kvf, kv_o, waits=fence1 + [("CC", cv)],
                     tag="kvin")
        vz = pb.op("vector", lambda h: h.memset(kvb, 0.0), waits=fence1,
                   inc="Vc", tag="kvz")
        vz2 = pb.op("vector", lambda h: h.memset(ksel, 0.0), waits=fence1,
                    inc="Vc", tag="kselz")
        for j in range(32):
            h_ = j // 2
            off = (h_ % 2) * 64
            pb.op("vector",
                  lambda h, j=j, off=off: h.tensor_copy(
                      kvbv[:, j, off:off + 64], kvfv[:, j, 0:64]),
                  waits=[lv2, ("Vc", vz)], inc="Vc", tag=f"kvp{j}")
            pb.op("vector",
                  lambda h, j=j, h_=h_: h.tensor_copy(
                      kselv[:, j, h_:h_ + 1], kvfv[:, j, 64:65]),
                  waits=[lv2, ("Vc", vz2)], inc="Vc", tag=f"ksl{j}")
        kvready = pb.val("Vc")
        for ci, (c0, cl) in enumerate(PIECES):
            lq2 = pb.dma("sync", qpcv[:, :, 0:cl],
                         qpsp.rearrange("(j p) t -> p j t", p=128)[
                             :, :, c0:c0 + cl],
                         waits=B_qpc.w_waits() + fence1, tag=f"qpl{ci}")
            B_qpc.wrote(*lq2)
            w0 = P.wait_w(6) + [("Vc", kvready), lq2]
            tv = None
            for j in range(32):
                tv = mm(P.tile(6, cl, p0=0, p1=16), kselv[:, j, :],
                        qpcv[:, j, 0:cl], start=(j == 0), stop=(j == 31),
                        waits=w0 if j == 0 else [],
                        inc="Tn" if j == 31 else None, tag=f"nm{ci}_{j}")
            P.wrote(6, "Tn", tv)
            vv = pb.op("vector",
                       lambda h, cl=cl: h.tensor_scalar_max(
                           nrm[0:16, 0:cl], P.tile(6, cl, p0=0, p1=16),
                           CLAMP),
                       waits=[("Tn", tv)] + B_nrm.w_waits(), inc="Vc",
                       tag=f"ncl{ci}")
            P.read(6, "Vc", vv)
            av = pb.op("scalar",
                       lambda h, cl=cl: h.activation(
                           out=nrm[0:16, 0:cl], in_=nrm[0:16, 0:cl],
                           func=AF.Ln),
                       waits=[("Vc", vv)], inc="Ac", tag=f"nln{ci}")
            av = pb.op("scalar",
                       lambda h, cl=cl: h.activation(
                           out=nrm[0:16, 0:cl], in_=nrm[0:16, 0:cl],
                           func=AF.Exp, scale=-1.0),
                       waits=[("Ac", av)], inc="Ac", tag=f"nex{ci}")
            B_nrm.wrote("Ac", av)
            sv = pb.dma("sync", rcpsp[:, 0:cl], nrm[0:16, 0:cl],
                        waits=[("Ac", av)] + B_rcp.w_waits(), tag=f"no{ci}")
            B_nrm.read(*sv)
            B_rcp.wrote(*sv)
            lastrb = None
            for m in range(NMT):
                bank = m % 6
                w0 = P.wait_w(bank) + [("Vc", kvready), lq2]
                tv = None
                for kk in range(4):
                    j = m * 4 + kk
                    tv = mm(P.tile(bank, cl), kvbv[:, j, :], qpcv[:, j, 0:cl],
                            start=(kk == 0), stop=(kk == 3),
                            waits=w0 if kk == 0 else [],
                            inc="Tn" if kk == 3 else None,
                            tag=f"ao{ci}m{m}_{kk}")
                P.wrote(bank, "Tn", tv)
                par = m % 2
                l1 = pb.dma("gpsimd", rep[0:64, par * 512:par * 512 + cl],
                            rcpsp[2 * m, 0:cl].partition_broadcast(64),
                            waits=[sv] + B_rep[par].w_waits(),
                            tag=f"rb{ci}m{m}a")
                l2 = pb.dma("gpsimd", rep[64:128, par * 512:par * 512 + cl],
                            rcpsp[2 * m + 1, 0:cl].partition_broadcast(64),
                            waits=[sv] + B_rep[par].w_waits(),
                            tag=f"rb{ci}m{m}b")
                B_rep[par].wrote(*l2)
                lastrb = l2
                vv = pb.op("vector",
                           lambda h, m=m, cl=cl, bank=bank, par=par:
                           h.tensor_mul(
                               attnbv[:, m, 0:cl], P.tile(bank, cl),
                               rep[:, par * 512:par * 512 + cl]),
                           waits=[("Tn", tv), l2] +
                           (B_attnb.w_waits() if m == 0 else []),
                           inc="Vc", tag=f"am{ci}m{m}")
                P.read(bank, "Vc", vv)
                B_rep[par].read("Vc", vv)
            B_rcp.read(*lastrb)
            B_attnb.wrote("Vc", pb.val("Vc"))
            attnb_ready = pb.val("Vc")
            for m in range(NMT):
                bank = 6 if m % 2 == 0 else 7
                w0 = P.wait_w(bank) + [lo, ("Vc", attnb_ready)]
                tv = None
                for k in range(NMT):
                    tv = mm(P.tile(bank, cl),
                            wov[:, k, m * 128:(m + 1) * 128],
                            attnbv[:, k, 0:cl], start=(k == 0),
                            stop=(k == NMT - 1),
                            waits=w0 if k == 0 else [],
                            inc="Tn" if k == NMT - 1 else None,
                            tag=f"wo{ci}m{m}k{k}")
                P.wrote(bank, "Tn", tv)
                vv = pb.op("vector",
                           lambda h, m=m, c0=c0, cl=cl, bank=bank:
                           h.scalar_tensor_tensor(
                               out=xresv[:, m, c0:c0 + cl],
                               in0=P.tile(bank, cl), scalar=boc[:, m:m + 1],
                               in1=xresv[:, m, c0:c0 + cl],
                               op0=ALU.add, op1=ALU.add),
                           waits=[("Tn", tv)], inc="Vc", tag=f"wr{ci}m{m}")
                P.read(bank, "Vc", vv)
            B_attnb.read("Tn", pb.val("Tn"))
            B_qpc.read("Tn", pb.val("Tn"))
        for n in ("wos", "qpc", "kvb", "ksel", "attnb", "rep", "kvf"):
            A.free(n)
        return pb.val("Vc")

    def emit_wsum(srcv3, nk, width, ws_off, deps, tag):
        stgv = wstg.rearrange("p (b c) -> p b c", b=2)
        free_v = [None, None]
        mmr = [0, 0]
        nparts = (width + 511) // 512
        assert nparts <= 8
        ws_tvs = {}
        npieces = (width + 2047) // 2048
        for k in range(nk):
            for pc in range(npieces):
                co = pc * 2048
                pwid = min(2048, width - co)
                sl = (k * npieces + pc) % 2
                w = list(deps) + ([free_v[sl]] if free_v[sl] else [])
                if mmr[sl]:
                    w += [("Tn", mmr[sl])]
                lv = pb.dma("sync", stgv[:, sl, 0:pwid],
                            srcv3[:, k, co:co + pwid],
                            waits=w, tag=f"{tag}i{k}_{pc}")
                free_v[sl] = lv
                for lp in range((pwid + 511) // 512):
                    part = (co // 512) + lp
                    pw = min(512, pwid - lp * 512)
                    wk = [lv]
                    if k == 0:
                        wk += P.wait_w(part)
                    tv = mm(P.tile(part, pw, p0=0, p1=1), onescolb,
                            stgv[:, sl, lp * 512:lp * 512 + pw],
                            start=(k == 0), stop=(k == nk - 1),
                            waits=wk, inc="Tn", tag=f"{tag}m{k}_{part}")
                    mmr[sl] = tv
                    if k == nk - 1:
                        ws_tvs[part] = (pw, tv)
                        P.wrote(part, "Tn", tv)
        last = 0
        for part, (pw, tv) in ws_tvs.items():
            vv = pb.op("vector",
                       lambda h, part=part, pw=pw: h.tensor_copy(
                           brs(ws_off + part * 512, pw),
                           P.tile(part, pw, p0=0, p1=1)),
                       waits=[("Tn", tv)], inc="Vc", tag=f"{tag}e{part}")
            P.read(part, "Vc", vv)
            last = vv
        return last

    def emit_ffn(idx, xdep, wvb, tag):
        wg1b = abf(f"{tag}wg1", NMT * D)
        wg1v = wg1b.rearrange("p (k c) -> p k c", k=NMT)
        lw = pb.dma("sync", wg1v, wg1.rearrange("(k p) c -> p k c", p=128),
                    waits=wvb, tag=f"{tag}wg1l")
        lastg = 0
        gh = af(f"{tag}gh", 512)
        ghacc = af(f"{tag}ga", NMT)
        ghtmp = af(f"{tag}gt", 1)
        B_gh = Buf()
        first = True
        for ci, (c0, cl) in enumerate(CHUNKS):
            if cl == 3:
                continue
            lnv, _ = emit_ln_chunk(ci, c0, cl, xdep, f"{tag}g{ci}")
            for m in range(NMT):
                bank = 3 + (m % 5)
                w0 = P.wait_w(bank) + [("Vc", lnv), lw]
                for k in range(NMT):
                    mm(P.tile(bank, cl), wg1v[:, k, m * 128:(m + 1) * 128],
                       xlncv[:, k, 0:cl], start=(k == 0), stop=False,
                       waits=w0 if k == 0 else [], tag=f"{tag}g{ci}m{m}k{k}")
                tv = mm(P.tile(bank, cl), brs(WSPOOL + 0 + m * 128, 128),
                        nmr(c0, cl), start=False, stop=True, inc="Tn",
                        waits=[("Vc", wsv_g)] if first else [],
                        tag=f"{tag}g{ci}m{m}r")
                first = False
                P.wrote(bank, "Tn", tv)
                av = pb.op("scalar",
                           lambda h, bank=bank, cl=cl, m=m: h.activation(
                               out=gh[:, 0:cl], in_=P.tile(bank, cl),
                               func=AF.Relu, bias=gb1c[:, m:m + 1]),
                           waits=[("Tn", tv)] + B_gh.w_waits(), inc="Ac",
                           tag=f"{tag}g{ci}e{m}")
                P.read(bank, "Ac", av)
                B_gh.wrote("Ac", av)
                vv = pb.op("vector",
                           lambda h, cl=cl: h.reduce_sum(
                               ghtmp[:, 0:1].unsqueeze(2),
                               gh[:, 0:cl].unsqueeze(1),
                               axis=mybir.AxisListType.X),
                           waits=[("Ac", av)], inc="Vc",
                           tag=f"{tag}g{ci}r{m}")
                B_gh.read("Vc", vv)
                if ci == 1:
                    vv = pb.op("vector",
                               lambda h, m=m: h.tensor_copy(
                                   ghacc[:, m:m + 1], ghtmp[:, 0:1]),
                               waits=[("Vc", vv)], inc="Vc",
                               tag=f"{tag}g{ci}a{m}")
                else:
                    vv = pb.op("vector",
                               lambda h, m=m: h.tensor_add(
                                   ghacc[:, m:m + 1], ghacc[:, m:m + 1],
                                   ghtmp[:, 0:1]),
                               waits=[("Vc", vv)], inc="Vc",
                               tag=f"{tag}g{ci}a{m}")
        ghb = af(f"{tag}ghb", NMT // 2).bitcast(BF16)
        vv = pb.op("vector", lambda h: h.tensor_copy(ghb[:, 0:NMT], ghacc),
                   waits=[("Vc", vv)], inc="Vc", tag=f"{tag}gc")
        w0 = P.wait_w(3) + [("Vc", vv)]
        tv = None
        for k in range(NMT):
            tv = mm(P.tile(3, 1, p0=0, p1=8), gw2sb[:, k, :],
                    ghb[:, k:k + 1], start=(k == 0), stop=(k == NMT - 1),
                    waits=w0 if k == 0 else [],
                    inc="Tn" if k == NMT - 1 else None, tag=f"{tag}gm{k}")
        P.wrote(3, "Tn", tv)
        av = pb.op("scalar",
                   lambda h: h.activation(out=gtmp8[0:8, 0:1],
                                          in_=P.tile(3, 1, p0=0, p1=8),
                                          func=AF.Identity),
                   waits=[("Tn", tv)], inc="Ac", tag=f"{tag}ge")
        P.read(3, "Ac", av)
        sv = pb.dma("sync", g_i[idx][0:8], gtmp8[0:8, 0:1],
                    waits=[("Ac", av)], tag=f"{tag}gout")
        cvg = pb.op("gpsimd",
                    lambda h: h.collective_compute(
                        "AllReduce", ALU.add, ins=[g_i[idx][:]],
                        outs=[g_o[idx][:]], replica_groups=GROUPS),
                    waits=[sv], inc="CC", tag=f"{tag}gAR")
        lg = pb.dma("sync", grepc, g_o[idx][0:8].partition_broadcast(128),
                    waits=[("CC", cvg)], tag=f"{tag}gin")
        gv = pb.op("vector",
                   lambda h: h.scalar_tensor_tensor(
                       out=gcol[idx], in0=grepc, scalar=1.0 / 8192.0,
                       in1=gb2rep, op0=ALU.mult, op1=ALU.add),
                   waits=[lg], inc="Vc", tag=f"{tag}gfin")
        giv = gisb.rearrange("p (e c) -> p e c", e=9)
        for e in range(8):
            vv = pb.op("vector",
                       lambda h, e=e: h.tensor_scalar_mul(
                           giv[:, e, :], I128b,
                           gcol[idx][:, e:e + 1]),
                       waits=[("Vc", gv)] + (B_gi.w_waits() if e == 0 else []),
                       inc="Vc", tag=f"{tag}gi{e}")
        vv = pb.op("vector",
                   lambda h: h.tensor_copy(giv[:, 8, :], I128b),
                   waits=[("Vc", vv)], inc="Vc", tag=f"{tag}gi8")
        B_gi.wrote("Vc", vv)
        giready = vv
        tb1b = abf(f"{tag}tb1", FF)
        tb2b = abf(f"{tag}tb2", D)
        brow1 = af(f"{tag}b1r", FF)[0:1, :]
        brow2 = af(f"{tag}b2r", D)[0:1, :]
        l1 = pb.dma("gpsimd", tb1b[0:8, :], Tb1i, tag=f"{tag}tb1l")
        l2 = pb.dma("gpsimd", tb2b[0:8, :], Tb2i, tag=f"{tag}tb2l")
        l3 = pb.dma("sync", brow1, b1i.unsqueeze(0), tag=f"{tag}b1l")
        l4 = pb.dma("sync", brow2, b2i.unsqueeze(0), tag=f"{tag}b2l")
        gcb = af(f"{tag}gcb", 4).bitcast(BF16)
        lgc = pb.dma("sync", gcol8[idx][0:8, 0:1],
                     g_o[idx][0:8].unsqueeze(1), waits=[("CC", cvg)],
                     tag=f"{tag}gcld")
        vv = pb.op("vector",
                   lambda h: h.scalar_tensor_tensor(
                       out=gcol8[idx][0:8, 0:1], in0=gcol8[idx][0:8, 0:1],
                       scalar=1.0 / 8192.0, in1=gb2col8[0:8, 0:1],
                       op0=ALU.mult, op1=ALU.add),
                   waits=[lgc], inc="Vc", tag=f"{tag}gc8")
        vv = pb.op("vector",
                   lambda h: h.tensor_copy(gcb[0:8, 0:1],
                                           gcol8[idx][0:8, 0:1]),
                   waits=[("Vc", vv)], inc="Vc", tag=f"{tag}gcb")
        mbevs = []
        for wi, (tbb, brw, nparts, dst_off) in enumerate((
                (tb1b, brow1, 8, 0), (tb2b, brow2, 2, FF))):
            for part in range(nparts):
                bank = 4 + (part % 4)
                wk = P.wait_w(bank) + [l1, l2, ("Vc", vv)]
                tv = mm(P.tile(bank, 512, p0=0, p1=1),
                        gcb[0:8, 0:1], tbb[0:8, part * 512:part * 512 + 512],
                        start=True, stop=True, waits=wk, inc="Tn",
                        tag=f"{tag}mb{wi}_{part}")
                P.wrote(bank, "Tn", tv)
                ev = pb.op("vector",
                           lambda h, bank=bank, brw=brw, part=part:
                           h.tensor_add(
                               brw[0:1, part * 512:part * 512 + 512],
                               brw[0:1, part * 512:part * 512 + 512],
                               P.tile(bank, 512, p0=0, p1=1)),
                           waits=[("Tn", tv), l3, l4], inc="Vc",
                           tag=f"{tag}mbe{wi}_{part}")
                P.read(bank, "Vc", ev)
                mbevs.append(ev)
        s1 = pb.dma("sync", mbsp[idx, 0:FF], brow1,
                    waits=[("Vc", mbevs[-1])], tag=f"{tag}mbo1")
        s2 = pb.dma("sync", mbsp[idx, FF:FF + D], brow2,
                    waits=[("Vc", mbevs[-1])], tag=f"{tag}mbo2")
        lm1 = pb.dma("sync", mb1c,
                     mbsp[idx, 0:FF].rearrange("(t p) -> p t", p=128),
                     waits=[s1], tag=f"{tag}mb1i", slow=True)
        lm2 = pb.dma("sync", mb2c,
                     mbsp[idx, FF:FF + D].rearrange("(t p) -> p t", p=128),
                     waits=[s2], tag=f"{tag}mb2i", slow=True)
        A.free(f"{tag}tb1"); A.free(f"{tag}tb2")
        A.free(f"{tag}b1r"); A.free(f"{tag}b2r"); A.free(f"{tag}gcb")
        twb = abf(f"{tag}twb", 2 * NMT * 1024)
        twv = twb.rearrange("p (s k c) -> p s k c", s=2, k=NMT)
        mwst = abf(f"{tag}mwst", 512)
        B_tw = [Buf(), Buf()]
        B_mwst = Buf()
        mwdmas = []
        for half in range(2):
            wt_e = wtw1 if half == 0 else wtw2
            w_last = ww1 if half == 0 else ww2
            out_off = half * D * FSL
            for cb in range(2):
                for e in range(9):
                    sl = e % 2
                    srcd = (wt_e[e] if e < 8 else w_last)
                    w = B_tw[sl].w_waits() + wvb
                    lv = pb.dma("sync", twv[:, sl, :, :],
                                srcd.rearrange("(k p) c -> p k c", p=128),
                                waits=w, tag=f"{tag}tw{half}{cb}e{e}")
                    B_tw[sl].wrote(*lv)
                    lastm = None
                    for m in range(NMT):
                        wk = [lv, ("Vc", giready)]
                        if e == 0:
                            wk += P.wait_w(m)
                        lastm = mm(P.tile(m, 512), giv[:, e, :],
                                   twv[:, sl, m, cb * 512:cb * 512 + 512],
                                   start=(e == 0), stop=(e == 8),
                                   waits=wk,
                                   inc="Tn" if (e == 8 or m == NMT - 1)
                                   else None,
                                   tag=f"{tag}w{half}{cb}e{e}m{m}")
                        if e == 8:
                            P.wrote(m, "Tn", lastm)
                    B_tw[sl].read("Tn", lastm)
                for m in range(NMT):
                    av = pb.op("scalar",
                               lambda h, m=m: h.activation(
                                   out=mwst[:, :], in_=P.tile(m, 512),
                                   func=AF.Identity),
                               waits=P.wait_r(m) + B_mwst.w_waits(),
                               inc="Ac", tag=f"{tag}we{half}{cb}m{m}")
                    P.read(m, "Ac", av)
                    B_mwst.wrote("Ac", av)
                    dst = mw_i[idx][out_off:out_off + D * FSL].rearrange(
                        "(r c) -> r c", c=FSL)[
                        m * 128:(m + 1) * 128, cb * 512:cb * 512 + 512]
                    sv = pb.dma("sync", dst, mwst[:, :],
                                waits=[("Ac", av)], tag=f"{tag}wo{half}{cb}m{m}")
                    B_mwst.read(*sv)
                    mwdmas.append(sv)
        A.free(f"{tag}twb"); A.free(f"{tag}mwst")
        wsv1 = emit_wsum(
            mw_i[idx][0:D * FSL].rearrange("(k p c) -> p k c", p=128, c=FSL),
            NMT, FSL, WSPOOL + 1024, mwdmas, f"{tag}ws")
        svw = pb.dma("sync", mw_i[idx][2 * D * FSL:2 * D * FSL + FSL],
                     brs(WSPOOL + 1024, FSL), waits=[("Vc", wsv1)],
                     tag=f"{tag}wso")
        cv = pb.op("gpsimd",
                   lambda h: h.collective_compute(
                       "AllGather", ALU.bypass, ins=[mw_i[idx][:]],
                       outs=[mw_o[idx][:, :]], replica_groups=GROUPS),
                   waits=mwdmas + [svw], inc="CC", tag=f"{tag}mwAG")
        wsld = []
        for qi in range(4):
            wsld.append(pb.dma(
                "sync", brs(WSPOOL + 1024 + qi * FSL, FSL),
                mw_o[idx][qi, 2 * D * FSL:2 * D * FSL + FSL].unsqueeze(0),
                waits=[("CC", cv)], tag=f"{tag}wsl{qi}"))
        usb = abf(f"{tag}usb", 32 * 512)
        usbv = usb.rearrange("p (j t) -> p j t", j=32)
        mw1b = abf(f"{tag}mw1b", 2 * NMT * 128)
        mw1v = mw1b.rearrange("p (s k c) -> p s k c", s=2, k=NMT)
        mw2b = abf(f"{tag}mw2b", 2 * 32 * 128)
        mw2v = mw2b.rearrange("p (s k c) -> p s k c", s=2, k=32)
        B_m1 = [Buf(), Buf()]
        B_m2 = [Buf(), Buf()]
        B_usb = [Buf() for _ in range(32)]
        chunks_b = CHUNKS if idx == 0 else CHUNKS[1:]
        mwo1 = [mw_o[idx][qi, 0:D * FSL].rearrange(
            "(k p f) -> p k f", p=128, f=FSL) for qi in range(4)]
        mwo2 = [mw_o[idx][qi, D * FSL:2 * D * FSL].rearrange(
            "(k p f) -> p k f", p=128, f=D) for qi in range(4)]
        for ci, (c0, cl) in enumerate(chunks_b):
            lnv, _ = emit_ln_chunk(0, c0, cl, xdep, f"{tag}u{ci}")
            for m32 in range(32):
                sl = m32 % 2
                qi_src = m32 // 8
                fcol = m32 % 8
                w = B_m1[sl].w_waits() + [("CC", cv)]
                lv = pb.dma("sync", mw1v[:, sl, :, :],
                            mwo1[qi_src][:, :, fcol * 128:(fcol + 1) * 128],
                            waits=w, tag=f"{tag}u{ci}l{m32}")
                B_m1[sl].wrote(*lv)
                bank = 3 + (m32 % 4)
                w0 = P.wait_w(bank) + [lv, ("Vc", lnv)] + (
                    wsld if m32 == 0 and ci == 0 else [])
                tv = None
                for k in range(NMT):
                    tv = mm(P.tile(bank, cl), mw1v[:, sl, k, :],
                            xlncv[:, k, 0:cl], start=(k == 0), stop=False,
                            waits=w0 if k == 0 else [],
                            tag=f"{tag}u{ci}m{m32}k{k}")
                tv = mm(P.tile(bank, cl),
                        brs(WSPOOL + 1024 + m32 * 128, 128), nmr(c0, cl),
                        start=False, stop=True, inc="Tn",
                        tag=f"{tag}u{ci}m{m32}r")
                P.wrote(bank, "Tn", tv)
                B_m1[sl].read("Tn", tv)
                av = pb.op("scalar",
                           lambda h, m32=m32, cl=cl, bank=bank:
                           h.activation(
                               out=usbv[:, m32, 0:cl], in_=P.tile(bank, cl),
                               func=AF.Gelu_apprx_tanh,
                               bias=mb1c[:, m32:m32 + 1]),
                           waits=[("Tn", tv), lm1] +
                           B_usb[m32].w_waits(),
                           inc="Ac", tag=f"{tag}u{ci}e{m32}")
                P.read(bank, "Ac", av)
                B_usb[m32].wrote("Ac", av)
            uready = pb.val("Ac")
            for m in range(NMT):
                sl = m % 2
                bank = 3 + (m % 4)
                wl = B_m2[sl].w_waits() + [("CC", cv)]
                lvs = []
                for qi in range(4):
                    lvs.append(pb.dma(
                        "sync", mw2v[:, sl, qi * 8:(qi + 1) * 8, :],
                        mwo2[qi][:, :, m * 128:(m + 1) * 128],
                        waits=wl if qi == 0 else [], tag=f"{tag}o{ci}l{m}_{qi}"))
                B_m2[sl].wrote(*lvs[-1])
                for lvq in lvs[:-1]:
                    B_m2[sl].wrote(*lvq)
                w0 = P.wait_w(bank) + lvs + [("Ac", uready)]
                tv = None
                for k32 in range(32):
                    tv = mm(P.tile(bank, cl), mw2v[:, sl, k32, :],
                            usbv[:, k32, 0:cl], start=(k32 == 0),
                            stop=(k32 == 31),
                            waits=w0 if k32 == 0 else [],
                            inc="Tn" if k32 == 31 else None,
                            tag=f"{tag}o{ci}m{m}k{k32}")
                P.wrote(bank, "Tn", tv)
                B_m2[sl].read("Tn", tv)
                for j in range(32):
                    B_usb[j].read("Tn", tv)
                vv = pb.op("vector",
                           lambda h, m=m, c0=c0, cl=cl, bank=bank:
                           h.scalar_tensor_tensor(
                               out=xresv[:, m, c0:c0 + cl],
                               in0=P.tile(bank, cl),
                               scalar=mb2c[:, m:m + 1],
                               in1=xresv[:, m, c0:c0 + cl],
                               op0=ALU.add, op1=ALU.add),
                           waits=[("Tn", tv), lm2], inc="Vc",
                           tag=f"{tag}o{ci}e{m}")
                P.read(bank, "Vc", vv)
        for n in (f"{tag}usb", f"{tag}mw1b", f"{tag}mw2b", f"{tag}gh",
                  f"{tag}ga", f"{tag}gt", f"{tag}ghb", f"{tag}wg1"):
            A.free(n)
        return pb.val("Vc")

    def emit_attn_b(xdep, wvb):
        x3sv = pb.dma("sync", x3sp.rearrange("(k p) t -> p k t", p=128),
                      xresv, waits=xdep, tag="Bx3sp")
        wsv_in = emit_wsum(winw.rearrange("(k p) c -> p k c", p=128),
                           NMT, 2 * DI, WSPOOL + 1024, wvb, "BWS")
        inwb = abf("Binw", 2 * NMT * 128)
        inwv = inwb.rearrange("p (s k c) -> p s k c", s=2, k=NMT)
        B_inw = [Buf(), Buf()]
        xprojsb = abf("Bxproj", NDT * 96).rearrange(
            "p (d c) -> p d c", d=NDT)
        lxp = pb.dma("gpsimd", xprojsb,
                     xprojT.rearrange("(d p) c -> p d c", p=128),
                     tag="Bxpld")
        xxe = af("Bxxe", 515)
        convt = af("Bcv", 512)
        xct = af("Bxc", 512)
        xcbt = abf("Bxcb", 512)
        ddl = abf("Bddl", 512)
        dtile = af("Bdt", 512)
        sp1 = af("Bsp1", 512)
        B_xxe = Buf(); B_cv = Buf(); B_xc = Buf(); B_xcb = Buf()
        B_dd = Buf(); B_dt = Buf(); B_sp1 = Buf()
        B_car3 = Buf(); B_ssum = Buf(); B_xdbl = Buf()
        vz = pb.op("vector", lambda h: h.memset(ssum[:, :], 0.0), inc="Vc",
                   tag="Bss0")
        B_ssum.wrote("Vc", vz)
        first_r1 = True
        for ci, (c0, cl) in enumerate(CHUNKS):
            lnv, _ = emit_ln_chunk(ci, c0, cl, xdep, f"B{ci}")
            is_halo = (cl == 3)
            for m in range(32):
                sl = m % 2
                w = B_inw[sl].w_waits() + wvb
                lv = pb.dma("sync", inwv[:, sl, :, :],
                            winw.rearrange("(k p) c -> p k c", p=128)[
                                :, :, m * 128:(m + 1) * 128],
                            waits=w, tag=f"Bw{ci}m{m}")
                B_inw[sl].wrote(*lv)
                bank = 3 + (m % 4)
                w0 = P.wait_w(bank) + [lv, ("Vc", lnv)]
                tv = None
                for k in range(NMT):
                    tv = mm(P.tile(bank, cl), inwv[:, sl, k, :],
                            xlncv[:, k, 0:cl], start=(k == 0), stop=False,
                            waits=w0 if k == 0 else [],
                            tag=f"Bp{ci}m{m}k{k}")
                tv = mm(P.tile(bank, cl),
                        brs(WSPOOL + 1024 + m * 128, 128), nmr(c0, cl),
                        start=False, stop=True, inc="Tn",
                        waits=[("Vc", wsv_in)] if first_r1 else [],
                        tag=f"Bp{ci}m{m}r")
                first_r1 = False
                P.wrote(bank, "Tn", tv)
                B_inw[sl].read("Tn", tv)
                d = m if m < 16 else m - 16
                if m < 16:
                    if is_halo:
                        vv = pb.op("vector",
                                   lambda h, bank=bank, d=d: h.tensor_copy(
                                       xxe[:, 0:3], P.tile(bank, 3)),
                                   waits=[("Tn", tv)] + B_xxe.w_waits(),
                                   inc="Vc", tag=f"Bhx{d}")
                        P.read(bank, "Vc", vv)
                        vv = pb.op("vector",
                                   lambda h, d=d: h.tensor_mul(
                                       carry3v[:, d, :], xxe[:, 0:3],
                                       halomc[:, 0:3]),
                                   waits=[("Vc", vv)] + B_car3.w_waits(),
                                   inc="Vc", tag=f"Bhm{d}")
                        B_car3.wrote("Vc", vv)
                        B_xxe.read("Vc", vv)
                        B_xxe.wrote("Vc", vv)
                        continue
                    vv = pb.op("vector",
                               lambda h, d=d: h.tensor_copy(
                                   xxe[:, 0:3], carry3v[:, d, :]),
                               waits=B_car3.r_waits() + B_xxe.w_waits() +
                               B_car3.w_waits(),
                               inc="Vc", tag=f"Bxc3{ci}_{d}")
                    vv = pb.op("vector",
                               lambda h, bank=bank, cl=cl: h.tensor_copy(
                                   xxe[:, 3:3 + cl], P.tile(bank, cl)),
                               waits=[("Tn", tv), ("Vc", vv)], inc="Vc",
                               tag=f"Bxe{ci}_{d}")
                    P.read(bank, "Vc", vv)
                    B_xxe.wrote("Vc", vv)
                    vv2 = pb.op("vector",
                                lambda h, d=d, cl=cl: h.tensor_copy(
                                    carry3v[:, d, :], xxe[:, cl:cl + 3]),
                                waits=[("Vc", vv)], inc="Vc",
                                tag=f"Bc3u{ci}_{d}")
                    B_car3.wrote("Vc", vv2)
                    B_xxe.read("Vc", vv2)
                    vv = pb.op("vector",
                               lambda h, d=d, cl=cl: h.tensor_scalar_mul(
                                   convt[:, 0:cl], xxe[:, 0:cl],
                                   cwv[:, d, 0:1]),
                               waits=[("Vc", vv)] + B_cv.w_waits(),
                               inc="Vc", tag=f"Bcv0{ci}_{d}")
                    for j in range(1, 4):
                        vv = pb.op("vector",
                                   lambda h, d=d, cl=cl, j=j:
                                   h.scalar_tensor_tensor(
                                       out=convt[:, 0:cl],
                                       in0=xxe[:, j:j + cl],
                                       scalar=cwv[:, d, j:j + 1],
                                       in1=convt[:, 0:cl],
                                       op0=ALU.mult, op1=ALU.add),
                                   waits=[("Vc", vv)], inc="Vc",
                                   tag=f"Bcv{j}{ci}_{d}")
                    B_cv.wrote("Vc", vv)
                    B_xxe.read("Vc", vv)
                    av = pb.op("scalar",
                               lambda h, d=d, cl=cl: h.activation(
                                   out=xct[:, 0:cl], in_=convt[:, 0:cl],
                                   func=AF.Silu, bias=cbcol[:, d:d + 1]),
                               waits=[("Vc", vv)] + B_xc.w_waits(),
                               inc="Ac", tag=f"Bsl{ci}_{d}")
                    B_cv.read("Ac", av)
                    B_xc.wrote("Ac", av)
                    sv = pb.dma("sync",
                                xcsp[d * 128:(d + 1) * 128, c0:c0 + cl],
                                xct[:, 0:cl], waits=[("Ac", av)],
                                tag=f"Bxcs{ci}_{d}")
                    B_xc.read(*sv)
                    vv = pb.op("vector",
                               lambda h, cl=cl: h.tensor_copy(
                                   xcbt[:, 0:cl], xct[:, 0:cl]),
                               waits=[("Ac", av)] + B_xcb.w_waits(),
                               inc="Vc", tag=f"Bxcb{ci}_{d}")
                    B_xc.read("Vc", vv)
                    B_xcb.wrote("Vc", vv)
                    wk = [("Vc", vv)]
                    if d == 0:
                        wk += P.wait_w(7) + [lxp]
                    tvx = mm(P.tile(7, cl, p0=0, p1=96),
                             xprojsb[:, d, :], xcbt[:, 0:cl],
                             start=(d == 0), stop=(d == 15),
                             waits=wk, inc="Tn", tag=f"Bxd{ci}_{d}")
                    B_xcb.read("Tn", tvx)
                    if d == 15:
                        P.wrote(7, "Tn", tvx)
                        av = pb.op("scalar",
                                   lambda h, c0=c0, cl=cl: h.activation(
                                       out=xdbl[:, c0:c0 + cl],
                                       in_=P.tile(7, cl, p0=0, p1=96),
                                       func=AF.Identity),
                                   waits=[("Tn", tvx)] + B_xdbl.w_waits(),
                                   inc="Ac", tag=f"Bxde{ci}")
                        P.read(7, "Ac", av)
                        B_xdbl.wrote("Ac", av)
                else:
                    if is_halo:
                        vv = pb.op("vector",
                                   lambda h, bank=bank: h.tensor_copy(
                                       convt[:, 0:3], P.tile(bank, 3)),
                                   waits=[("Tn", tv)] + B_cv.w_waits(),
                                   inc="Vc", tag=f"Bhr{d}")
                        P.read(bank, "Vc", vv)
                        B_cv.wrote("Vc", vv)
                        continue
                    av = pb.op("scalar",
                               lambda h, bank=bank, cl=cl: h.activation(
                                   out=xct[:, 0:cl], in_=P.tile(bank, cl),
                                   func=AF.Silu),
                               waits=[("Tn", tv)] + B_xc.w_waits(),
                               inc="Ac", tag=f"Bsr{ci}_{d}")
                    P.read(bank, "Ac", av)
                    B_xc.wrote("Ac", av)
                    sv = pb.dma("sync",
                                sresp[d * 128:(d + 1) * 128, c0:c0 + cl],
                                xct[:, 0:cl], waits=[("Ac", av)],
                                tag=f"Bsrs{ci}_{d}")
                    B_xc.read(*sv)
            if is_halo:
                continue
            avx = pb.op("vector",
                        lambda h, c0=c0, cl=cl: h.tensor_copy(
                            ddl[0:64, 0:cl], xdbl[0:64, c0:c0 + cl]),
                        waits=B_xdbl.r_waits() + B_dd.w_waits(), inc="Vc",
                        tag=f"Bdc{ci}")
            B_xdbl.read("Vc", avx)
            B_dd.wrote("Vc", avx)
            for d in range(NDT):
                bank = 3 + (d % 3)
                w0 = P.wait_w(bank) + [("Vc", avx)]
                tv = mm(P.tile(bank, cl), dtwsb[:, d * 128:(d + 1) * 128],
                        ddl[0:64, 0:cl], True, True, waits=w0, inc="Tn",
                        tag=f"Bdm{ci}_{d}")
                P.wrote(bank, "Tn", tv)
                B_dd.read("Tn", tv)
                a1 = pb.op("scalar",
                           lambda h, bank=bank, cl=cl, d=d: h.activation(
                               out=sp1[:, 0:cl], in_=P.tile(bank, cl),
                               func=AF.Abs, bias=dtbcol[:, d:d + 1]),
                           waits=[("Tn", tv)] + B_sp1.w_waits(), inc="Ac",
                           tag=f"Bsa{ci}_{d}")
                a2 = pb.op("scalar",
                           lambda h, cl=cl: h.activation(
                               out=sp1[:, 0:cl], in_=sp1[:, 0:cl],
                               func=AF.Exp, scale=-1.0),
                           waits=[], inc="Ac", tag=f"Bse{ci}_{d}")
                a3 = pb.op("scalar",
                           lambda h, cl=cl: h.activation(
                               out=sp1[:, 0:cl], in_=sp1[:, 0:cl],
                               func=AF.Ln, bias=1.0),
                           waits=[], inc="Ac", tag=f"Bsn{ci}_{d}")
                B_sp1.wrote("Ac", a3)
                a4 = pb.op("scalar",
                           lambda h, bank=bank, cl=cl, d=d: h.activation(
                               out=dtile[:, 0:cl], in_=P.tile(bank, cl),
                               func=AF.Relu, bias=dtbcol[:, d:d + 1]),
                           waits=B_dt.w_waits(), inc="Ac",
                           tag=f"Bsr2{ci}_{d}")
                P.read(bank, "Ac", a4)
                vv = pb.op("vector",
                           lambda h, cl=cl: h.tensor_add(
                               dtile[:, 0:cl], dtile[:, 0:cl],
                               sp1[:, 0:cl]),
                           waits=[("Ac", a4)], inc="Vc", tag=f"Bsf{ci}_{d}")
                B_sp1.read("Vc", vv)
                B_dt.wrote("Vc", vv)
                sv = pb.dma("sync", dsp[d * 128:(d + 1) * 128, c0:c0 + cl],
                            dtile[:, 0:cl], waits=[("Vc", vv)],
                            tag=f"Bds{ci}_{d}")
                B_dt.read(*sv)
                vv = pb.op("vector",
                           lambda h, d=d, cl=cl: h.reduce_sum(
                               sp1[:, 0:1], dtile[:, 0:cl],
                               axis=mybir.AxisListType.X),
                           waits=[("Vc", vv)] + B_sp1.w_waits(), inc="Vc",
                           tag=f"Bsu{ci}_{d}")
                B_dt.read("Vc", vv)
                B_sp1.wrote("Vc", vv)
                vv = pb.op("vector",
                           lambda h, d=d: h.tensor_add(
                               ssum[:, d:d + 1], ssum[:, d:d + 1],
                               sp1[:, 0:1]),
                           waits=[("Vc", vv)] + B_ssum.w_waits(),
                           inc="Vc", tag=f"Bsq{ci}_{d}")
                B_sp1.read("Vc", vv)
                B_ssum.wrote("Vc", vv)
        bv1 = pb.dma("sync", bcsp[0], xdbl[64:80, :],
                     waits=B_xdbl.r_waits(), tag="Bbc0")
        bv2 = pb.dma("sync", bcsp[1], xdbl[80:96, :],
                     waits=B_xdbl.r_waits(), tag="Bbc1")
        for n in ("Binw", "Bxxe", "Bcv", "Bxc", "Bxcb", "Bddl", "Bdt",
                  "Bsp1", "Bxproj"):
            A.free(n)
        form_f = ([("Tn", pb.val("Tn")), ("Vc", pb.val("Vc")),
                   ("Ac", pb.val("Ac"))] + pb.dfence("dS") +
                  pb.dfence("dG"))
        pv = None
        for d in range(NDT):
            vv = pb.op("vector",
                       lambda h, d=d: h.tensor_scalar_mul(
                           pcolv[:, d, :], Acolv[:, d, :],
                           ssum[:, d:d + 1]),
                       waits=B_ssum.r_waits() if d == 0 else [],
                       inc="Vc", tag=f"Bpc{d}")
            pv = vv
        av = pb.op("scalar",
                   lambda h: h.activation(out=pcol, in_=pcol, func=AF.Exp),
                   waits=[("Vc", pv)], inc="Ac", tag="Bpce")
        NQT = 4
        TQ = 512
        brep = xres[:, 0:NDT * TQ].rearrange("p (n t) -> p n t", n=16)
        crep = xres[:, NDT * TQ:2 * NDT * TQ].rearrange(
            "p (n t) -> p n t", n=16)
        B_brep = Buf(); B_crep = Buf()
        B_brep.read("Tn", pb.val("Tn"))
        B_brep.read("Vc", pb.val("Vc"))
        dtl = af("SdT", 2 * 512)
        dtlv = dtl.rearrange("p (s t) -> p s t", s=2)
        xctl = af("SxcT", 2 * 512)
        xctlv = xctl.rearrange("p (s t) -> p s t", s=2)
        wtl = af("SwT", 512)
        dab = af("SdA", 2 * (TQ + 1))
        dbb = af("SdB", 2 * (TQ + 1))
        hbb = af("Sh", 2 * (TQ + 1))
        carry = af("Scr", NDT * NST // 16 * 16)
        carryv = carry.rearrange("p (d n) -> p d n", d=NDT)
        B_dtl = [Buf(), Buf()]; B_xctl = [Buf(), Buf()]; B_wtl = Buf()
        B_dab = Buf(); B_dbb = Buf(); B_hbb = Buf(); B_hcb = Buf()
        B_ytl = Buf(); B_yft = Buf(); B_srt = Buf(); B_gy = Buf()
        B_wo = [Buf(), Buf()]
        B_x3t = [Buf(), Buf()]; B_x4 = [Buf(), Buf()]; B_carry = Buf()

        def scan_phase(phase):
            nonlocal bv1, bv2
            azp = pb.op("vector",
                        lambda h: h.memset(
                            dab.rearrange("p (j c) -> p j c", j=2)[
                                :, :, 0:1], 0.0),
                        waits=(form_f if phase == "A" else []) +
                        B_dab.w_waits(), inc="Vc", tag=f"S{phase}az")
            B_dab.wrote("Vc", azp)
            if phase == "A":
                cw0 = pb.op("vector", lambda h: h.memset(carry, 0.0),
                            waits=form_f + B_carry.w_waits(), inc="Vc",
                            tag="Scz")
            else:
                cw0 = pb.op("vector",
                            lambda h: h.tensor_copy(carry, hinc),
                            waits=B_carry.w_waits() + [("Vc", hin_v)],
                            inc="Vc", tag="Sci")
            B_carry.wrote("Vc", cw0)
            for qt in range(NQT):
                t0 = HALO + qt * TQ
                lb = []
                for n in range(NST):
                    lb.append(pb.dma(
                        "gpsimd", brep[:, n, :],
                        bcsp[0, n, t0:t0 + TQ].partition_broadcast(128),
                        waits=([bv1] + B_brep.w_waits()) if n == 0 else [],
                        tag=f"S{phase}b{qt}_{n}"))
                    lb.append(pb.dma(
                        "gpsimd", crep[:, n, :],
                        bcsp[1, n, t0:t0 + TQ].partition_broadcast(128),
                        waits=([bv2] + B_crep.w_waits()) if n == 0 else [],
                        tag=f"S{phase}c{qt}_{n}"))
                for x in lb[0::2]:
                    B_brep.wrote(*x)
                for x in lb[1::2]:
                    B_crep.wrote(*x)
                for d in range(NDT):
                    dsl = d % 2
                    ld = pb.dma("sync", dtlv[:, dsl, :],
                                dsp[d * 128:(d + 1) * 128, t0:t0 + TQ],
                                waits=B_dtl[dsl].w_waits(),
                                tag=f"S{phase}d{qt}_{d}")
                    B_dtl[dsl].wrote(*ld)
                    lx = pb.dma("sync", xctlv[:, dsl, :],
                                xcsp[d * 128:(d + 1) * 128, t0:t0 + TQ],
                                waits=B_xctl[dsl].w_waits(),
                                tag=f"S{phase}x{qt}_{d}")
                    B_xctl[dsl].wrote(*lx)
                    wv_ = pb.op("vector",
                                lambda h, dsl=dsl: h.tensor_mul(
                                    wtl, dtlv[:, dsl, :], xctlv[:, dsl, :]),
                                waits=[ld, lx] + B_wtl.w_waits(), inc="Vc",
                                tag=f"S{phase}w{qt}_{d}")
                    B_dtl[dsl].read("Vc", wv_); B_xctl[dsl].read("Vc", wv_)
                    B_wtl.wrote("Vc", wv_)
                    for q in range(8):
                        avs = []
                        for j in range(2):
                            n = 2 * q + j
                            avs.append(pb.op(
                                "scalar",
                                lambda h, j=j, n=n, dsl=dsl: h.activation(
                                    out=dab.rearrange(
                                        "p (jj c) -> p jj c", jj=2)[
                                        :, j, 1:TQ + 1],
                                    in_=dtlv[:, dsl, :], func=AF.Exp,
                                    scale=Acolv[:, d, n:n + 1]),
                                waits=[ld] + B_dab.w_waits(), inc="Ac",
                                tag=f"S{phase}da{qt}_{d}_{q}_{j}"))
                        B_dtl[dsl].read("Ac", avs[-1])
                        B_dab.wrote("Ac", avs[-1])
                        bz = pb.op("vector",
                                   lambda h, d=d, q=q: h.tensor_copy(
                                       dbb.rearrange("p (j c) -> p j c",
                                                     j=2)[:, :, 0:1],
                                       carryv[:, d, 2 * q:2 * q + 2]
                                       .unsqueeze(2)),
                                   waits=B_dbb.w_waits() +
                                   B_carry.r_waits(),
                                   inc="Vc", tag=f"S{phase}bz{qt}_{d}_{q}")
                        bu = pb.op("vector",
                                   lambda h, q=q: h.tensor_mul(
                                       dbb.rearrange("p (j c) -> p j c",
                                                     j=2)[:, :, 1:TQ + 1],
                                       wtl.unsqueeze(1).broadcast_to(
                                           [128, 2, TQ]),
                                       brep[:, 2 * q:2 * q + 2, :]),
                                   waits=[("Vc", bz), ("Vc", wv_),
                                          lb[4 * q], lb[4 * q + 2]],
                                   inc="Vc", tag=f"S{phase}bu{qt}_{d}_{q}")
                        B_wtl.read("Vc", bu)
                        B_brep.read("Vc", bu)
                        B_dbb.wrote("Vc", bu)
                        sc = pb.op("vector",
                                   lambda h: h.tensor_tensor_scan(
                                       hbb[:, :], dab[:, :], dbb[:, :],
                                       0.0, op0=ALU.mult, op1=ALU.add),
                                   waits=[("Vc", bu), ("Ac", avs[-1])] +
                                   B_hbb.w_waits(),
                                   inc="Vc", tag=f"S{phase}sc{qt}_{d}_{q}")
                        B_dab.read("Vc", sc); B_dbb.read("Vc", sc)
                        B_hbb.wrote("Vc", sc)
                        cu = pb.op("vector",
                                   lambda h, d=d, q=q: h.tensor_copy(
                                       carryv[:, d, 2 * q:2 * q + 2],
                                       hbb.rearrange("p (j c) -> p j c",
                                                     j=2)[:, :, TQ:TQ + 1]
                                       .squeeze(2)),
                                   waits=[("Vc", sc)] + B_carry.w_waits(),
                                   inc="Vc", tag=f"S{phase}cu{qt}_{d}_{q}")
                        B_carry.wrote("Vc", cu)
                        B_hbb.read("Vc", cu)
                        if phase == "C":
                            hview = hbb.rearrange(
                                "p (j c) -> p j c", j=2)[:, :, 1:TQ + 1]
                            hc = pb.op(
                                "vector",
                                lambda h, q=q: h.tensor_tensor(
                                    hcb.rearrange("p (c j) -> p c j", j=2),
                                    hview.transpose([0, 2, 1]),
                                    crep[:, 2 * q:2 * q + 2, :]
                                    .transpose([0, 2, 1]),
                                    op=ALU.mult),
                                waits=[("Vc", sc), lb[4 * q + 1],
                                       lb[4 * q + 3]] + B_hcb.w_waits(),
                                inc="Vc", tag=f"S{phase}hc{qt}_{d}_{q}")
                            B_hbb.read("Vc", hc)
                            B_crep.read("Vc", hc)
                            B_hcb.wrote("Vc", hc)
                            if q == 0:
                                yv = pb.op(
                                    "vector",
                                    lambda h: h.reduce_sum(
                                        ytl[:, :].unsqueeze(2),
                                        hcb.rearrange(
                                            "p (c j) -> p c j", j=2),
                                        axis=mybir.AxisListType.X),
                                    waits=[("Vc", hc)] + B_ytl.w_waits(),
                                    inc="Vc", tag=f"S{phase}y{qt}_{d}_{q}")
                            else:
                                yv = pb.op(
                                    "vector",
                                    lambda h: h.reduce_sum(
                                        yft[:, :].unsqueeze(2),
                                        hcb.rearrange(
                                            "p (c j) -> p c j", j=2),
                                        axis=mybir.AxisListType.X),
                                    waits=[("Vc", hc)] + B_yft.w_waits(),
                                    inc="Vc", tag=f"S{phase}yr{qt}_{d}_{q}")
                                yv = pb.op(
                                    "vector",
                                    lambda h: h.tensor_add(
                                        ytl[:, :], ytl[:, :], yft[:, :]),
                                    waits=[("Vc", yv)], inc="Vc",
                                    tag=f"S{phase}ya{qt}_{d}_{q}")
                            B_hcb.read("Vc", yv)
                            B_ytl.wrote("Vc", yv)
                            B_yft.wrote("Vc", yv)
                    if phase == "C":
                        lr = pb.dma("sync", srt[:, :],
                                    sresp[d * 128:(d + 1) * 128,
                                          t0:t0 + TQ],
                                    waits=B_srt.w_waits(),
                                    tag=f"SCr{qt}_{d}")
                        B_srt.wrote(*lr)
                        fv = pb.op("vector",
                                   lambda h, d=d, dsl=dsl:
                                   h.scalar_tensor_tensor(
                                       out=yft[:, :],
                                       in0=xctlv[:, dsl, :],
                                       scalar=Dcol[:, d:d + 1],
                                       in1=ytl[:, :], op0=ALU.mult,
                                       op1=ALU.add),
                                   waits=B_ytl.r_waits() + [("Vc", wv_)] +
                                   B_yft.w_waits(),
                                   inc="Vc", tag=f"SCyf{qt}_{d}")
                        B_xctl[dsl].read("Vc", fv)
                        B_ytl.read("Vc", fv)
                        B_yft.wrote("Vc", fv)
                        gv_ = pb.op("vector",
                                    lambda h, d=d: h.tensor_mul(
                                        gyv[:, d, :], yft[:, :],
                                        srt[:, :]),
                                    waits=[("Vc", fv), lr] +
                                    (B_gy.w_waits() if d == 0 else []),
                                    inc="Vc", tag=f"SCgy{qt}_{d}")
                        B_srt.read("Vc", gv_)
                        B_yft.read("Vc", gv_)
                if phase == "C":
                    B_gy.wrote("Vc", pb.val("Vc"))
                    for m in range(NMT):
                        sl = m % 2
                        w = B_wo[sl].w_waits() + wvb
                        lw = pb.dma("sync", wov2[:, sl, :, :],
                                    woutw.rearrange("(k p) c -> p k c",
                                                    p=128)[
                                        :, :, m * 128:(m + 1) * 128],
                                    waits=w, tag=f"SCw{qt}m{m}")
                        B_wo[sl].wrote(*lw)
                        lx3 = pb.dma("sync", x3tv[:, sl, :],
                                     x3sp[m * 128:(m + 1) * 128,
                                          t0:t0 + TQ],
                                     waits=B_x3t[sl].w_waits() + [x3sv],
                                     tag=f"SCx3{qt}m{m}")
                        B_x3t[sl].wrote(*lx3)
                        bank = 3 + (m % 4)
                        w0 = P.wait_w(bank) + [lw] + B_gy.r_waits() + [
                            ("Vc", pb.val("Vc"))]
                        tv = None
                        for k in range(NDT):
                            tv = mm(P.tile(bank, TQ), wov2[:, sl, k, :],
                                    gyv[:, k, :], start=(k == 0),
                                    stop=(k == NDT - 1),
                                    waits=w0 if k == 0 else [],
                                    inc="Tn" if k == NDT - 1 else None,
                                    tag=f"SCo{qt}m{m}k{k}")
                        P.wrote(bank, "Tn", tv)
                        B_wo[sl].read("Tn", tv)
                        B_gy.read("Tn", tv)
                        vv = pb.op("vector",
                                   lambda h, sl=sl, bank=bank: h.tensor_add(
                                       x4sv[:, sl, :], P.tile(bank, TQ),
                                       x3tv[:, sl, :]),
                                   waits=[("Tn", tv), lx3] +
                                   B_x4[sl].w_waits(),
                                   inc="Vc", tag=f"SCe{qt}m{m}")
                        P.read(bank, "Vc", vv)
                        B_x3t[sl].read("Vc", vv)
                        s4 = pb.dma("sync",
                                    x4sp[m * 128:(m + 1) * 128,
                                         t0:t0 + TQ],
                                    x4sv[:, sl, :], waits=[("Vc", vv)],
                                    tag=f"SCx4{qt}m{m}")
                        B_x4[sl].wrote(*s4)
                        B_x4[sl].read(*s4)
            return pb.val("Vc")

        scan_phase("A")
        av2 = pb.val("Ac")
        vv = pb.op("vector", lambda h: h.tensor_copy(vagg, carry),
                   waits=B_carry.r_waits() + B_carry.w_waits(), inc="Vc",
                   tag="Bvag")
        s1 = pb.dma("sync",
                    cv_i[0:DI * NST].rearrange("(p c) -> p c", p=128),
                    pcol, waits=[("Ac", av2)], tag="Bcv1")
        s2 = pb.dma("sync",
                    cv_i[DI * NST:2 * DI * NST].rearrange(
                        "(p c) -> p c", p=128),
                    vagg, waits=[("Vc", vv)], tag="Bcv2")
        cv = pb.op("gpsimd",
                   lambda h: h.collective_compute(
                       "AllGather", ALU.bypass, ins=[cv_i[:]],
                       outs=[cv_o[:, :]], replica_groups=GROUPS),
                   waits=[s1, s2], inc="CC", tag="BcvAG")
        hq = af("Bhq", 256 * 3)
        hv = af("Bhv", 256 * 3)
        lds = []
        for i in range(3):
            lds.append(pb.dma("sync", hq[:, i * 256:(i + 1) * 256],
                              cv_o[i + 1, 0:DI * NST].rearrange(
                                  "(p c) -> p c", p=128),
                              waits=[("CC", cv)], tag=f"Bhq{i}"))
            lds.append(pb.dma("sync", hv[:, i * 256:(i + 1) * 256],
                              cv_o[i, DI * NST:2 * DI * NST].rearrange(
                                  "(p c) -> p c", p=128),
                              waits=[("CC", cv)], tag=f"Bhv{i}"))
        vv = pb.op("vector",
                   lambda h: h.tensor_mul(hq[:, 0:256], hq[:, 0:256],
                                          hv[:, 0:256]),
                   waits=lds, inc="Vc", tag="BH2a")
        vv = pb.op("vector",
                   lambda h: h.tensor_add(hq[:, 0:256], hq[:, 0:256],
                                          hv[:, 256:512]),
                   waits=[("Vc", vv)], inc="Vc", tag="BH2b")
        vv = pb.op("vector",
                   lambda h: h.tensor_mul(hq[:, 256:512], hq[:, 256:512],
                                          hq[:, 0:256]),
                   waits=[("Vc", vv)], inc="Vc", tag="BH3a")
        vv = pb.op("vector",
                   lambda h: h.tensor_add(hq[:, 256:512], hq[:, 256:512],
                                          hv[:, 512:768]),
                   waits=[("Vc", vv)], inc="Vc", tag="BH3b")
        vv = pb.op("vector",
                   lambda h: h.tensor_scalar_mul(hinc, hv[:, 0:256],
                                                 selc[:, 1:2]),
                   waits=[("Vc", vv)], inc="Vc", tag="Bhi1")
        vv = pb.op("vector",
                   lambda h: h.scalar_tensor_tensor(
                       out=hinc, in0=hq[:, 0:256], scalar=selc[:, 2:3],
                       in1=hinc, op0=ALU.mult, op1=ALU.add),
                   waits=[("Vc", vv)], inc="Vc", tag="Bhi2")
        hin_v = pb.op("vector",
                      lambda h: h.scalar_tensor_tensor(
                          out=hinc, in0=hq[:, 256:512],
                          scalar=selc[:, 3:4],
                          in1=hinc, op0=ALU.mult, op1=ALU.add),
                      waits=[("Vc", vv)], inc="Vc", tag="Bhi3")
        A.free("Bhq"); A.free("Bhv")
        hcb = af("Shc", 2 * TQ)
        ytl = af("Sy", TQ)
        yft = af("Syf", TQ)
        srt = af("Ssr", TQ)
        gyall = abf("Sgy", NDT * TQ)
        gyv = gyall.rearrange("p (d t) -> p d t", d=NDT)
        wob = abf("Swo", 2 * NDT * 128)
        wov2 = wob.rearrange("p (s k c) -> p s k c", s=2, k=NDT)
        x3t = af("Sx3", 2 * TQ)
        x3tv = x3t.rearrange("p (s t) -> p s t", s=2)
        x4st = af("Sx4", 2 * TQ)
        x4sv = x4st.rearrange("p (s t) -> p s t", s=2)
        scan_phase("C")
        for n in ("SdT", "SxcT", "SwT", "SdA", "SdB", "Sh", "Shc", "Sy",
                  "Syf", "Ssr", "Sgy", "Swo", "Sx3", "Sx4", "Scr"):
            A.free(n)
        return pb.val("Vc")

    setupw, cstv = emit_setup()
    wjobs = [
        wjob(Wq, wqkvo[0], NMT, D, WSPOOL + 1024, "wq"),
        wjob(Wk, wqkvo[1], NMT, D, WSPOOL + 2048, "wk"),
        wjob(Wv, wqkvo[2], NMT, D, WSPOOL + 3072, "wv"),
        wjob(Wo, wqkvo[3], NMT, D, None, "woo"),
        wjob(gw1T, wg1, NMT, D, WSPOOL + 0, "wg1c"),
    ]
    wv_a, wsv_a = emit_wcast(cstv, wjobs)
    wsv_g = wsv_a
    x2v = emit_attn_a(wv_a, setupw, cstv, wsv_a)
    wjobs_b = [
        wjob(inwT, winw, NMT, 2 * DI, None, "winw"),
        wjob(outwT, woutw, NDT, D, None, "woutw"),
        wjob(W1Ts, ww1, NMT, FSL, None, "w1c"),
        wjob(W2Ts, ww2, FSL // 128, D, None, "w2c"),
    ]
    for e in range(8):
        wjobs_b.append((lambda k, e=e: TW1Ts[e, k * 128:(k + 1) * 128, :],
                        lambda k, e=e: wtw1[e, k * 128:(k + 1) * 128, :],
                        NMT, FSL, None, f"t1c{e}"))
        wjobs_b.append((lambda k, e=e: TW2Ts[e, k * 128:(k + 1) * 128, :],
                        lambda k, e=e: wtw2[e, k * 128:(k + 1) * 128, :],
                        FSL // 128, D, None, f"t2c{e}"))
    wv_b, _ = emit_wcast(cstv, wjobs_b)
    x3v = emit_ffn(0, [("Vc", x2v)], wv_b, "F1")
    if debug_stage not in ("x2", "x3", "qsb", "ws", "g", "mw", "mb", "qp",
                           "kv"):
        x4v = emit_attn_b([("Vc", x3v)], wv_b)
    if debug_stage is None:
        rx = pb.dma("sync", xresv[:, :, HALO:],
                    x4sp.rearrange("(k p) t -> p k t", p=128)[:, :, HALO:],
                    waits=[("Vc", x4v), ("Tn", pb.val("Tn")),
                           ("Ac", pb.val("Ac"))] + pb.dfence("dS") +
                    pb.dfence("dG"), tag="x4rl")
        x5v = emit_ffn(1, [rx], wv_b, "F2")
        sv = pb.dma("sync", x5T.rearrange("(d p) t -> p d t", p=128),
                    xresv[:, :, HALO:], waits=[("Vc", x5v)], tag="out")
        pb.op("gpsimd", lambda h: h.engine_nop(), waits=[sv], tag="final")

    if debug_stage == "x2":
        sv = pb.dma("sync", x5T.rearrange("(d p) t -> p d t", p=128),
                    xresv[:, :, HALO:], waits=[("Vc", x2v)], tag="dbgx2")
        pb.op("gpsimd", lambda h: h.engine_nop(), waits=[sv],
              tag="final")
    elif debug_stage in ("qp", "kv"):
        if debug_stage == "qp":
            dbg = nc.dram_tensor("dbg", [HEADS * FM, T], BF16,
                                 kind="ExternalOutput").ap()
            qv = qpsp.rearrange("(j p) t -> p j t", p=128)
            dv = dbg.rearrange("(j p) t -> p j t", p=128)
            pv = [("Vc", x2v)]
            for j in range(32):
                li = pb.dma("sync", wstg[:, 0:T], qv[:, j, :],
                            waits=pv, tag=f"dq{j}")
                pv = [pb.dma("sync", dv[:, j, :], wstg[:, 0:T],
                             waits=[li], tag=f"dq2{j}")]
            sv = pv[0]
        else:
            dbg = nc.dram_tensor("dbg", [128, 32 * 65], F32,
                                 kind="ExternalOutput").ap()
            w32 = wstg.bitcast(F32)
            li = pb.dma("sync", w32[:, 0:32 * 65], kv_o,
                        waits=[("Vc", x2v)], tag="dkv")
            sv = pb.dma("sync", dbg, w32[:, 0:32 * 65],
                        waits=[li], tag="dkv2")
        pb.op("gpsimd", lambda h: h.engine_nop(), waits=[sv],
              tag="final")
    elif debug_stage == "x3":
        sv = pb.dma("sync", x5T.rearrange("(d p) t -> p d t", p=128),
                    xresv[:, :, HALO:], waits=[("Vc", x3v)], tag="dbgx3")
        pb.op("gpsimd", lambda h: h.engine_nop(), waits=[sv],
              tag="final")
    elif debug_stage == "x4":
        w32 = wstg.bitcast(F32)
        pv = [("Vc", x4v)]
        for k in range(NMT):
            li = pb.dma("sync", w32[:, 0:TOK],
                        x4sp[k * 128:(k + 1) * 128, HALO:], waits=pv,
                        tag=f"dx4a{k}")
            pv = [pb.dma("sync", x5T[k * 128:(k + 1) * 128, :],
                         w32[:, 0:TOK], waits=[li], tag=f"dx4b{k}")]
        pb.op("gpsimd", lambda h: h.engine_nop(), waits=pv, tag="final")
    elif debug_stage == "qsb":
        pb.op("gpsimd", lambda h: h.engine_nop(), waits=[("Vc", x2v)],
              tag="final")
    elif debug_stage == "mb":
        dbg = nc.dram_tensor("dbg", [128, 40], F32, kind="ExternalOutput").ap()
        li = pb.dma("sync", dbg[:, 0:32], mb1c, waits=[("Vc", x3v)], tag="dmba")
        sv = pb.dma("sync", dbg[:, 32:40], mb2c, waits=[("Vc", x3v)], tag="dmbb")
        pb.op("gpsimd", lambda h: h.engine_nop(), waits=[li, sv], tag="final")
    elif debug_stage == "mw":
        dbg = nc.dram_tensor("dbg", [D, 2 * FSL + 1], BF16,
                             kind="ExternalOutput").ap()
        w32v = wstg
        pv = [("Vc", x3v)]
        for k in range(NMT):
            li = pb.dma("sync", w32v[:, 0:FSL],
                        mw_i[0][0:D * FSL].rearrange(
                            "(k p c) -> p k c", p=128, c=FSL)[:, k, :],
                        waits=pv, tag=f"dmw{k}")
            li2 = pb.dma("sync", w32v[:, FSL:2 * FSL],
                         mw_i[0][D * FSL:2 * D * FSL].rearrange(
                             "(k p c) -> p k c", p=128, c=D)[:, k, :],
                         waits=[li], tag=f"dmwb{k}")
            pv = [pb.dma("sync",
                         dbg.rearrange("(k p) c -> p k c", p=128)[:, k, 0:2 * FSL],
                         w32v[:, 0:2 * FSL], waits=[li2], tag=f"dmw2{k}")]
        sv = pv[0]
        pb.op("gpsimd", lambda h: h.engine_nop(), waits=[sv], tag="final")
    elif debug_stage == "g":
        dbg = nc.dram_tensor("dbg", [128, 8], F32, kind="ExternalOutput").ap()
        sv = pb.dma("sync", dbg, gcol[0], waits=[("Vc", x3v)], tag="dbgg")
        pb.op("gpsimd", lambda h: h.engine_nop(), waits=[sv], tag="final")
    elif debug_stage == "ws":
        dbg = nc.dram_tensor("dbg", [1, 5120], BF16,
                             kind="ExternalOutput").ap()
        sv = pb.dma("sync", dbg, brs(WSPOOL, 5120), waits=[("Vc", x2v)],
                    tag="dbgws")
        pb.op("gpsimd", lambda h: h.engine_nop(), waits=[sv],
              tag="final")
    pb.check()

    sems = {}
    with contextlib.ExitStack() as es:
        for s in pb.cnt:
            sems[s] = es.enter_context(nc.semaphore(s))
        with nc.Block() as blk:
            pb.run(blk, sems)
    stack.close()
    return nc


_BUILT = {}


def _get_nc(debug_stage):
    key = debug_stage or "full"
    if key not in _BUILT:
        _BUILT[key] = build(debug_stage)
    return _BUILT[key]


def kernel(**inputs):
    I = inputs
    x0 = np.asarray(I["L0_attn_a_x"], np.float32)

    def f32(a):
        return np.ascontiguousarray(np.asarray(a, np.float32))

    shared = {
        "Wq": f32(I["L0_attn_a_Wq"]), "Wk": f32(I["L0_attn_a_Wk"]),
        "Wv": f32(I["L0_attn_a_Wv"]), "Wo": f32(I["L0_attn_a_Wo"]),
        "bq": f32(I["L0_attn_a_bq"]), "bk": f32(I["L0_attn_a_bk"]),
        "bv": f32(I["L0_attn_a_bv"]), "bo": f32(I["L0_attn_a_bo"]),
        "RM": f32(I["L0_attn_a_random_matrix"]),
        "inwT": f32(np.asarray(I["L0_attn_b_in_w"], np.float32).T),
        "convw": f32(np.asarray(I["L0_attn_b_conv_w"],
                                np.float32).reshape(DI, 4)),
        "convb": f32(I["L0_attn_b_conv_b"]),
        "xprojT": f32(np.asarray(I["L0_attn_b_xproj_w"], np.float32).T),
        "dtwT": f32(np.asarray(I["L0_attn_b_dt_w"], np.float32).T),
        "dtb": f32(I["L0_attn_b_dt_b"]),
        "Alog": f32(I["L0_attn_b_A_log"]),
        "Dvec": f32(I["L0_attn_b_D"]),
        "outwT": f32(np.asarray(I["L0_attn_b_out_w"], np.float32).T),
        "gw1T": f32(np.asarray(I["L0_ffn_gate_w1"], np.float32).T),
        "gw2T": f32(np.asarray(I["L0_ffn_gate_w2"], np.float32).T),
        "gb1": f32(I["L0_ffn_gate_b1"]), "gb2": f32(I["L0_ffn_gate_b2"]),
        "b1": f32(I["L0_ffn_b1"]), "b2": f32(I["L0_ffn_b2"]),
        "Tb1": f32(I["L0_ffn_Tb1"]), "Tb2": f32(I["L0_ffn_Tb2"]),
        "I128": np.eye(128, dtype=np.float32),
    }
    W1T = np.asarray(I["L0_ffn_W1"], np.float32).T
    W2T = np.asarray(I["L0_ffn_W2"], np.float32).T
    TW1T = np.asarray(I["L0_ffn_TW1"], np.float32).transpose(0, 2, 1)
    TW2T = np.asarray(I["L0_ffn_TW2"], np.float32).transpose(0, 2, 1)

    in_maps = []
    for c in range(8):
        b = c // 4
        qi = c % 4
        t0 = qi * TOK
        xs = np.zeros((T, D), np.float32)
        xs[HALO:] = x0[b, t0:t0 + TOK]
        if qi > 0:
            xs[:HALO] = x0[b, t0 - HALO:t0]
        hm = np.zeros((128, 4), np.float32)
        hm[:, :3] = 0.0 if qi == 0 else 1.0
        hm[:, 3] = 1.0
        sel = np.zeros((128, 4), np.float32)
        sel[:, qi] = 1.0
        fs = qi * FSL
        m = dict(shared)
        m["x0T"] = np.ascontiguousarray(xs.T)
        m["halom"] = hm
        m["sel"] = sel
        m["W1Ts"] = np.ascontiguousarray(W1T[:, fs:fs + FSL])
        m["W2Ts"] = np.ascontiguousarray(W2T[fs:fs + FSL, :])
        m["TW1Ts"] = np.ascontiguousarray(TW1T[:, :, fs:fs + FSL])
        m["TW2Ts"] = np.ascontiguousarray(TW2T[:, fs:fs + FSL, :])
        in_maps.append(m)

    nc = _get_nc(DEBUG_STAGE)
    res = run_bass_kernel_spmd(nc, in_maps, core_ids=list(range(8)))
    out = np.empty((2, 8192, D), np.float32)
    for c in range(8):
        b = c // 4
        t0 = (c % 4) * TOK
        out[b, t0:t0 + TOK] = res.results[c]["x5T"].T
    return out
